# version 21
# baseline (speedup 1.0000x reference)
import numpy as np
import ml_dtypes
import concourse.bass as bass
import concourse.mybir as mybir
from concourse.bass_utils import run_bass_kernel_spmd
from contextlib import ExitStack

F32 = mybir.dt.float32
BF16 = mybir.dt.bfloat16
U8 = mybir.dt.uint8
AF = mybir.ActivationFunctionType
ALU = mybir.AluOpType
AXX = mybir.AxisListType.X
BFNP = ml_dtypes.bfloat16

B, T, D, HD, NE, NLAY = 16, 2048, 1024, 2048, 8, 2
NCORE = 8
BL = B // NCORE
NCD = D // 128
NCH = HD // 128
TTOK = 512
NT = T // TTOK
NTC = T // 128
DEC = 0.95
EPS = 1e-5
INV_D = 1.0 / D
RING = 8448
RSLOT = 64

ENGS = ("pe", "dve", "act", "pool", "sp")
NWQ, NSQ = 16, 8
SEMS = ("pe", "dve", "act", "pl") + tuple(f"wq{i}" for i in range(NWQ)) + \
    tuple(f"sq{i}" for i in range(NSQ))


def make_plan():
    p = []
    for l in range(NLAY):
        for e in range(NCD):
            p.append(("kvr", l, e, 3 * 1024))
        for do in range(NCD):
            p.append(("wo", l, do, 1024))
    p.append(("router", 0, 0, 128))
    for ex in range(-1, NE):
        for h1 in range(NCH):
            p.append(("w13", ex, h1, 2048))
        for do in range(NCD):
            p.append(("w2", ex, do, 2048))
    return p


def _tile128(mat):
    K, C = mat.shape
    n = K // 128
    return mat.reshape(n, 128, C).transpose(1, 0, 2).reshape(128, n * C)


def pack_stream(ins):
    Wk, Wv, Wr, Wo = (np.asarray(ins[k], np.float32) for k in
                      ("L0_attn_Wk", "L0_attn_Wv", "L0_attn_Wr", "L0_attn_Wo"))
    rw = np.asarray(ins["L0_ffn_router_w"], np.float32)
    w1 = np.asarray(ins["L0_ffn_w1"], np.float32)
    w3 = np.asarray(ins["L0_ffn_w3"], np.float32)
    w2 = np.asarray(ins["L0_ffn_w2"], np.float32)
    sw1 = np.asarray(ins["L0_ffn_sw1"], np.float32)
    sw3 = np.asarray(ins["L0_ffn_sw3"], np.float32)
    sw2 = np.asarray(ins["L0_ffn_sw2"], np.float32)
    WkT = [Wk[l].T.copy() for l in range(NLAY)]
    WvT = [Wv[l].T.copy() for l in range(NLAY)]
    WrT = [Wr[l].T.copy() for l in range(NLAY)]
    WoT = [Wo[l].T.copy() for l in range(NLAY)]
    RT = rw.T.copy()
    cols = []
    for kind, a, b, n in make_plan():
        if kind == "kvr":
            l, e = a, b
            sl = slice(e * 128, (e + 1) * 128)
            cols.append(np.concatenate(
                [_tile128(WkT[l][:, sl]), _tile128(WvT[l][:, sl]),
                 _tile128(WrT[l][:, sl])], axis=1))
        elif kind == "wo":
            l, do = a, b
            cols.append(_tile128(WoT[l][:, do * 128:(do + 1) * 128]))
        elif kind == "router":
            rt = np.zeros((128, 128), np.float32)
            rt[:, :64] = _tile128(RT)
            cols.append(rt)
        elif kind == "w13":
            ex, h1 = a, b
            m1 = sw1[0] if ex < 0 else w1[ex]
            m3 = sw3[0] if ex < 0 else w3[ex]
            sl = slice(h1 * 128, (h1 + 1) * 128)
            cols.append(np.concatenate(
                [_tile128(m1[:, sl]), _tile128(m3[:, sl])], axis=1))
        elif kind == "w2":
            ex, do = a, b
            m2 = sw2[0] if ex < 0 else w2[ex]
            cols.append(_tile128(m2[:, do * 128:(do + 1) * 128]))
    ws = np.concatenate(cols, axis=1)
    return np.ascontiguousarray(ws)


class Reg:
    __slots__ = ("w", "r")

    def __init__(self):
        self.w = None
        self.r = {}


class Prog:
    def __init__(self):
        self.q = {e: [] for e in ENGS}
        self.cnt = {s: 0 for s in SEMS}
        self.waited = {e: {} for e in ENGS}
        self._rot = {"wq": 0, "sq": 0}

    def dma_sem(self, queue):
        n = {"wq": NWQ, "sq": NSQ}[queue]
        i = self._rot[queue]
        self._rot[queue] = (i + 1) % n
        s = f"{queue}{i}"
        if self.cnt[s] > 0:
            eng = {"wq": "pool", "sq": "sp"}[queue]
            self.wait(eng, {s: self.cnt[s]}, force=True)
        return s

    def op(self, eng, emit, reads=(), writes=(), sem=None, inc=0, rmark=None,
           war=()):
        deps = {}
        for r in reads:
            if r.w:
                deps[r.w[0]] = max(deps.get(r.w[0], 0), r.w[1])
        for w in writes:
            if w.w:
                deps[w.w[0]] = max(deps.get(w.w[0], 0), w.w[1])
            for s, c in w.r.items():
                deps[s] = max(deps.get(s, 0), c)
        for w in war:
            if w.w:
                deps[w.w[0]] = max(deps.get(w.w[0], 0), w.w[1])
            for s, c in w.r.items():
                deps[s] = max(deps.get(s, 0), c)
        own = {"pe": "pe", "sp": "sq", "pool": "wq"}.get(eng)
        if own is not None:
            for s in [s for s in deps if s.startswith(own) or s == own]:
                deps.pop(s)
        wt = self.waited[eng]
        waits = []
        for s in sorted(deps):
            c = deps[s]
            if c > 0 and wt.get(s, 0) < c:
                waits.append((s, c))
                wt[s] = c
        if sem is not None and inc:
            self.cnt[sem] += inc
            mark = (sem, self.cnt[sem], inc)
        else:
            mark = None
        self.q[eng].append((emit, waits, mark))
        rm = (mark[0], mark[1]) if mark else rmark
        if rm:
            for r in reads:
                r.r[rm[0]] = max(r.r.get(rm[0], 0), rm[1])
            for w in writes:
                w.w = rm
                w.r = {}
        return mark

    def wait(self, eng, deps, force=False):
        own = {"pe": "pe", "sp": "sq", "pool": "wq"}.get(eng)
        if own is not None and not force:
            for s in [s for s in deps if s.startswith(own) or s == own]:
                deps.pop(s)
        wt = self.waited[eng]
        waits = []
        for s in sorted(deps):
            c = deps[s]
            if c > 0 and wt.get(s, 0) < c:
                waits.append((s, c))
                wt[s] = c
        if waits:
            self.q[eng].append((None, waits, None))

    def fence(self, src_regs, dst_regs):
        deps = {}
        for r in src_regs:
            if r.w:
                deps[r.w[0]] = max(deps.get(r.w[0], 0), r.w[1])
            for s, c in r.r.items():
                deps[s] = max(deps.get(s, 0), c)
        for d in dst_regs:
            for s, c in deps.items():
                d.r[s] = max(d.r.get(s, 0), c)


DEBUG_TAPS = False

def build_nc(debug=False):
    nc = bass.Bass("TRN2", target_bir_lowering=False)
    plan = make_plan()
    WTOT = sum(it[3] for it in plan)
    taps = {}
    if debug:
        for nmx, shp, dt in (("ks7", [128, T], BF16), ("kv7", [128, T], F32),
                             ("st7", [128, T], BF16), ("sr7", [128, T], BF16),
                             ("h0", [128, NCD * T], F32), ("hl0", [128, NCD * T], F32),
                             ("hl1", [128, NCD * T], F32), ("x1", [128, NCD * T], F32),
                             ("nm0", [128, NCD * T], BF16), ("nm2", [128, NCD * T], BF16),
                             ("cwa", [128, NTC * NE], F32), ("hsh", [128, NCD * T], F32),
                             ("y0", [128, NCD * T], BF16)):
            taps[nmx] = nc.dram_tensor("tap_" + nmx, shp, dt, kind="ExternalOutput").ap()

    xin = nc.dram_tensor("xin", [BL, D, T], F32, kind="ExternalInput").ap()
    ws = nc.dram_tensor("wstream", [128, WTOT], F32, kind="ExternalInput").ap()
    ident_in = nc.dram_tensor("ident", [128, 128], F32, kind="ExternalInput").ap()
    lnw_in = nc.dram_tensor("lnwcols", [NLAY, 128, NCD], F32, kind="ExternalInput").ap()
    lnb_in = nc.dram_tensor("lnbcols", [NLAY, 128, NCD], F32, kind="ExternalInput").ap()
    bias_in = nc.dram_tensor("biasrow", [1, NE], F32, kind="ExternalInput").ap()
    xout = nc.dram_tensor("xout", [BL, D, T], F32, kind="ExternalOutput").ap()
    scrD = nc.dram_tensor("scrD", [128, 128], F32, kind="Internal").ap()

    es = ExitStack()
    Hb = nc.alloc_sbuf_tensor("Hb", [128, NCD * T], F32)
    NM = nc.alloc_sbuf_tensor("NM", [128, NCD * T], BF16)
    ring = nc.alloc_sbuf_tensor("ring", [128, RING], BF16)
    ident = nc.alloc_sbuf_tensor("ident_s", [128, 128], F32)
    cw_all = nc.alloc_sbuf_tensor("cw_all", [128, NTC * NE], F32)
    cwT = nc.alloc_sbuf_tensor("cwT", [128, 128], F32)
    onesrow = nc.alloc_sbuf_tensor("onesrow", [1, 128], F32)
    onesrowb = nc.alloc_sbuf_tensor("onesrowb", [1, 128], BF16)
    onescolb = nc.alloc_sbuf_tensor("onescolb", [128, 1], BF16)
    dec_t = nc.alloc_sbuf_tensor("dec_t", [128, 1], F32)
    epsc = nc.alloc_sbuf_tensor("epsc", [128, 1], F32)
    nc.const_aps.register(F32, EPS, epsc.ap())
    wcols = nc.alloc_sbuf_tensor("wcols", [128, NLAY * NCD], F32)
    bcols = nc.alloc_sbuf_tensor("bcols", [128, NLAY * NCD], F32)
    biasrow = nc.alloc_sbuf_tensor("biasrow_s", [1, NE], F32)
    biasbc = nc.alloc_sbuf_tensor("biasbc", [128, NE], F32)
    lg_all = nc.alloc_sbuf_tensor("lg_all", [128, NTC * NE], F32)
    m16a = nc.alloc_sbuf_tensor("m16a", [128, NTC], F32)
    m16b = nc.alloc_sbuf_tensor("m16b", [128, NTC], F32)
    rw1_16 = nc.alloc_sbuf_tensor("rw1_16", [128, NTC], F32)
    rw2_16 = nc.alloc_sbuf_tensor("rw2_16", [128, NTC], F32)
    mask1a = nc.alloc_sbuf_tensor("mask1a", [128, NTC * NE], F32)
    maskda = nc.alloc_sbuf_tensor("maskda", [128, NTC * NE], F32)
    tmpa = nc.alloc_sbuf_tensor("tmpa", [128, NTC * NE], F32)
    xb = nc.alloc_sbuf_tensor("xb", [128, 2 * TTOK], BF16)
    x2b = nc.alloc_sbuf_tensor("x2b", [128, 2 * TTOK], BF16)
    t1f = nc.alloc_sbuf_tensor("t1f", [128, TTOK], F32)
    t2f = nc.alloc_sbuf_tensor("t2f", [128, TTOK], F32)

    arena = nc.alloc_sbuf_tensor("arena", [128, 81920], U8)
    base = nc.lookup_mloc(arena).addr
    at = nc.alloc_sbuf_tensor_at
    eo_t = at("eo", [128, NCH * T], BF16, offset=base + 0)
    y_t = at("yt", [128, NCD * T], BF16, offset=base + 0)
    kv_t = at("kvt", [128, T], F32, offset=base + 32768)
    absb = [at(f"absb{i}", [128, TTOK], F32, offset=base + 32768 + i * 2048)
            for i in range(2)]
    bbsb = [at(f"bbsb{i}", [128, TTOK], F32, offset=base + 36864 + i * 2048)
            for i in range(2)]
    bfsb = [at(f"bfsb{i}", [128, TTOK], F32, offset=base + 36864 + i * 2048)
            for i in range(2)]
    afsb = [at(f"afsb{i}", [128, TTOK], F32, offset=base + 40960 + i * 2048)
            for i in range(2)]
    st_t = at("stt", [128, T], BF16, offset=base + 40960)
    sr_t = at("srt", [128, T], BF16, offset=base + 45056)
    ks_t = at("kst", [128, T], BF16, offset=base + 49152)
    r_sum = at("r_sum", [1, TTOK], F32, offset=base + 53248)
    r_sq = at("r_sq", [1, TTOK], F32, offset=base + 55296)
    r_t = at("r_t", [1, TTOK], F32, offset=base + 57344)
    r_rstd2 = [at(f"r_rstd{i}", [1, TTOK], F32, offset=base + 59392 + i * 2048)
               for i in range(2)]
    r_nmr2 = [at(f"r_nmr{i}", [1, TTOK], F32, offset=base + 63488 + i * 2048)
              for i in range(2)]
    xstr = at("xstr", [128, T], F32, offset=base + 67584)
    xstr_b = at("xstr_b", [128, T], F32, offset=base + 45056)
    silu_t = [at(f"silu{i}", [128, TTOK], BF16, offset=base + 67584 + i * 1024)
              for i in range(2)]
    tmpm = [at(f"tmpm{i}", [128, TTOK], F32, offset=base + 69632 + i * 2048)
            for i in range(2)]
    cw_row = at("cw_row", [1, T], BF16, offset=base + 73728)
    cw_bc = at("cw_bc", [128, T], BF16, offset=base + 77824)

    pb = [nc.alloc_psum_tensor(f"pb{i}", [128, 512], F32) for i in range(8)]

    sems = {}
    for s in SEMS:
        sems[s] = es.enter_context(nc.semaphore(s))

    p = Prog()

    RH = [[Reg() for _ in range(NT)] for _ in range(NCD)]
    RN = [[Reg() for _ in range(NT)] for _ in range(NCD)]
    RY = [Reg() for _ in range(NCD)]
    REO = [Reg() for _ in range(NCH)]
    RB = [Reg() for _ in range(8)]
    Rkv, Rst, Rsr, Rks = Reg(), Reg(), Reg(), Reg()
    Rrows = [Reg() for _ in range(3)]
    Rrstd2 = [Reg(), Reg()]
    Rnmr2 = [Reg(), Reg()]
    Rxstr, Rxstrb, Rt1, Rt2 = Reg(), Reg(), Reg(), Reg()
    Rxb = [Reg(), Reg()]
    Rx2b = [Reg(), Reg()]
    Rabsb = [Reg(), Reg()]
    Rbbsb = [Reg(), Reg()]
    Rbfsb = [Reg(), Reg()]
    Rafsb = [Reg(), Reg()]
    Rsilu = [Reg(), Reg()]
    Rtmpm = [Reg(), Reg()]
    Rcwrow, Rcwbc, Rcwall, RcwT, RscrD = Reg(), Reg(), Reg(), Reg(), Reg()
    Rdec, Reps, Rones, Ronesb, Ronescol = Reg(), Reg(), Reg(), Reg(), Reg()
    Rident, Rbrow, Rbbc = Reg(), Reg(), Reg()
    Rwcols = [Reg() for _ in range(NLAY)]
    Rbcols = [Reg() for _ in range(NLAY)]
    Rtop = Reg()
    Rlog = Reg()

    nslot = RING // RSLOT
    ring_slot_free = [Reg() for _ in range(nslot)]

    def H(c, tt=None):
        a = Hb.ap()
        return a[:, c * T:(c + 1) * T] if tt is None else \
            a[:, c * T + tt * TTOK: c * T + (tt + 1) * TTOK]

    def nm(c, tt=None):
        a = NM.ap()
        if tt is None:
            return a[:, c * T:(c + 1) * T]
        return a[:, c * T + tt * TTOK: c * T + (tt + 1) * TTOK]

    def nm128(c, tc):
        return NM.ap()[:, c * T + tc * 128: c * T + tc * 128 + 128]

    def eo(h1, tt):
        return eo_t.ap()[:, h1 * T + tt * TTOK: h1 * T + (tt + 1) * TTOK]

    def yv(e, tt=None):
        a = y_t.ap()
        if tt is None:
            return a[:, e * T:(e + 1) * T]
        return a[:, e * T + tt * TTOK: e * T + (tt + 1) * TTOK]

    ring_pos = [0]

    def ring_alloc(ncols, soff):
        segs = []
        left = ncols
        cur = soff
        while left > 0:
            pos = ring_pos[0]
            n = min(left, RING - pos)
            segs.append((pos, n, cur))
            ring_pos[0] = (pos + n) % RING
            left -= n
            cur += n
        return segs

    def emit_piece_dma(ncols, soff):
        segs = ring_alloc(ncols, soff)
        regs = []
        for (roff, n, so) in segs:
            sreg = Reg()
            s0 = roff // RSLOT
            s1 = (roff + n - 1) // RSLOT
            prev, seen = [], set()
            for i in range(s0, s1 + 1):
                r0 = ring_slot_free[i]
                if r0 is not None and id(r0) not in seen:
                    seen.add(id(r0))
                    prev.append(r0)
                    assert r0.r or r0.w is None, (
                        "ring WAR hazard: overwriting unconsumed piece")

            def emit(h, roff=roff, n=n, so=so):
                return h.dma_start(out=ring.ap()[:, roff:roff + n],
                                   in_=ws[:, so:so + n])
            p.op("pool", emit, reads=[], writes=[sreg], war=prev,
                 sem=p.dma_sem("wq"), inc=16)
            for i in range(s0, s1 + 1):
                ring_slot_free[i] = sreg
            regs.append(sreg)
        base_off = segs[0][0]
        contig = len(segs) == 1

        def tile(a, b):
            if contig:
                return ring.ap()[:, base_off + a: base_off + b]
            acc = 0
            for (roff, n, _so) in segs:
                if a >= acc and b <= acc + n:
                    return ring.ap()[:, roff + (a - acc): roff + (b - acc)]
                acc += n
            raise AssertionError("ring tile spans wrap; adjust RSLOT/RING")
        return tile, regs

    def group(bank, outw, pieces):
        n = len(pieces)
        endmark = ("pe", p.cnt["pe"] + 1)
        for i, (lt, rh, regs) in enumerate(pieces):
            first, last = i == 0, i == n - 1
            m = lt.shape[-1]

            def emit(h, lt=lt, rh=rh, first=first, last=last, bank=bank, m=m, outw=outw):
                return h.matmul(pb[bank].ap()[0:m, 0:outw], lt, rh,
                                start=first, stop=last)
            if last:
                p.op("pe", emit, reads=list(regs), writes=[RB[bank]], sem="pe", inc=1)
            else:
                p.op("pe", emit, reads=list(regs),
                     writes=[RB[bank]] if first else [], rmark=endmark)
        return endmark

    def tap(name, src_t, regs):
        if not debug or name not in taps:
            return
        p.op("sp", lambda h: h.dma_start(out=taps[name], in_=src_t),
             reads=regs, writes=[], sem=p.dma_sem("sq"), inc=16)

    def init():
        p.op("dve", lambda h: h.memset(dec_t.ap(), DEC), writes=[Rdec], sem="dve", inc=1)
        p.op("dve", lambda h: h.memset(epsc.ap(), EPS), writes=[Reps], sem="dve", inc=1)
        p.op("dve", lambda h: h.memset(onesrow.ap(), 1.0), writes=[Rones], sem="dve", inc=1)
        p.op("dve", lambda h: h.memset(onesrowb.ap(), 1.0), writes=[Ronesb], sem="dve", inc=1)
        p.op("dve", lambda h: h.memset(onescolb.ap(), 1.0), writes=[Ronescol], sem="dve", inc=1)
        p.op("sp", lambda h: h.dma_start(out=ident.ap(), in_=ident_in),
             writes=[Rident], sem=p.dma_sem("sq"), inc=16)
        for l in range(NLAY):
            p.op("sp", lambda h, l=l: h.dma_start(
                out=wcols.ap()[:, l * NCD:(l + 1) * NCD], in_=lnw_in[l]),
                writes=[Rwcols[l]], sem=p.dma_sem("sq"), inc=16)
            p.op("sp", lambda h, l=l: h.dma_start(
                out=bcols.ap()[:, l * NCD:(l + 1) * NCD], in_=lnb_in[l]),
                writes=[Rbcols[l]], sem=p.dma_sem("sq"), inc=16)
        p.op("sp", lambda h: h.dma_start(out=biasrow.ap(), in_=bias_in),
             writes=[Rbrow], sem=p.dma_sem("sq"), inc=16)
        p.op("pe", lambda h: h.matmul(pb[0].ap()[:, 0:NE], onesrow.ap(),
                                      biasrow.ap(), start=True, stop=True),
             reads=[Rones, Rbrow], writes=[RB[0]], sem="pe", inc=1)
        p.op("act", lambda h: h.activation(out=biasbc.ap(), in_=pb[0].ap()[:, 0:NE],
                                           func=AF.Copy),
             reads=[RB[0]], writes=[Rbbc], sem="act", inc=1)

    def emit_ln(kind, l, seq, src, src_regs_fn):
        nmout = kind != "ln1"

        def stats(tt):
            bs, bq = 0 + (tt % 2), 2 + (tt % 2)
            for c in range(NCD):
                sl = (c % 2)
                x2a = x2b.ap()[:, sl * TTOK:(sl + 1) * TTOK]
                sregs = src_regs_fn(c, tt)
                if nmout:
                    cast_dst, cast_reg = nm(c, tt), RN[c][tt]
                else:
                    cast_dst = xb.ap()[:, sl * TTOK:(sl + 1) * TTOK]
                    cast_reg = Rxb[sl]
                p.op("act", lambda h, c=c, tt=tt, cast_dst=cast_dst:
                     h.activation(out=cast_dst, in_=src(c, tt), func=AF.Copy),
                     reads=sregs, writes=[cast_reg], sem="act", inc=1)
                p.op("dve", lambda h, x2a=x2a, cast_dst=cast_dst: h.tensor_mul(
                    x2a, cast_dst, cast_dst),
                    reads=[cast_reg], writes=[Rx2b[sl]], sem="dve", inc=1)
                first, last = c == 0, c == NCD - 1
                p.op("pe", lambda h, first=first, last=last, bs=bs,
                     cast_dst=cast_dst: h.matmul(
                         pb[bs].ap()[0:1, :], onescolb.ap(), cast_dst,
                         start=first, stop=last),
                     reads=[cast_reg, Ronescol],
                     writes=[RB[bs]] if (first or last) else [],
                     sem="pe", inc=1)
                p.op("pe", lambda h, first=first, last=last, bq=bq, x2a=x2a:
                     h.matmul(pb[bq].ap()[0:1, :], onescolb.ap(), x2a,
                              start=first, stop=last),
                     reads=[Rx2b[sl], Ronescol],
                     writes=[RB[bq]] if (first or last) else [],
                     sem="pe", inc=1)

        def rows(tt):
            bs, bq, pr = 0 + (tt % 2), 2 + (tt % 2), tt % 2
            rstd_a, nmr_a = r_rstd2[pr].ap(), r_nmr2[pr].ap()
            p.op("act", lambda h, bs=bs, nmr_a=nmr_a: h.activation(
                out=nmr_a, in_=pb[bs].ap()[0:1, :], func=AF.Copy,
                scale=INV_D),
                reads=[RB[bs]], writes=[Rnmr2[pr]], sem="act", inc=1)
            p.op("act", lambda h, bq=bq: h.activation(
                out=r_sq.ap(), in_=pb[bq].ap()[0:1, :], func=AF.Copy,
                scale=INV_D),
                reads=[RB[bq]], writes=[Rrows[1]], sem="act", inc=1)
            p.op("dve", lambda h, nmr_a=nmr_a: h.tensor_mul(
                r_t.ap(), nmr_a, nmr_a),
                reads=[Rnmr2[pr]], writes=[Rrows[2]], sem="dve", inc=1)
            p.op("dve", lambda h: h.tensor_sub(r_t.ap(), r_sq.ap(), r_t.ap()),
                 reads=[Rrows[1], Rrows[2]], writes=[Rrows[2]], sem="dve", inc=1)
            p.op("act", lambda h, rstd_a=rstd_a: h.activation(
                out=rstd_a, in_=r_t.ap(), func=AF.Sqrt, bias=EPS),
                reads=[Rrows[2], Reps], writes=[Rrstd2[pr]], sem="act", inc=1)

        def bca(tt):
            ba, bb, pr = 4 + (tt % 2), 6 + (tt % 2), tt % 2
            rstd_a, nmr_a = r_rstd2[pr].ap(), r_nmr2[pr].ap()
            p.op("pe", lambda h, ba=ba, rstd_a=rstd_a: h.matmul(
                pb[ba].ap(), onesrow.ap(), rstd_a, start=True, stop=True),
                reads=[Rrstd2[pr], Rones], writes=[RB[ba]], sem="pe", inc=1)
            p.op("pe", lambda h, bb=bb, nmr_a=nmr_a: h.matmul(
                pb[bb].ap(), onesrow.ap(), nmr_a, start=True, stop=True),
                reads=[Rnmr2[pr], Rones], writes=[RB[bb]], sem="pe", inc=1)
            if nmout:
                p.op("dve", lambda h, ba=ba, pr=pr: h.reciprocal(
                    absb[pr].ap(), pb[ba].ap()),
                    reads=[RB[ba]], writes=[Rabsb[pr]], sem="dve", inc=1)
                p.op("dve", lambda h, bb=bb, pr=pr: h.scalar_tensor_tensor(
                    bbsb[pr].ap(), pb[bb].ap(), -1.0, absb[pr].ap(),
                    ALU.mult, ALU.mult),
                    reads=[RB[bb], Rabsb[pr]], writes=[Rbbsb[pr]],
                    sem="dve", inc=1)
                rn_tt = [RN[c][tt] for c in range(NCD)]

                def nmv(tt=tt):
                    return NM.ap().rearrange("p (c t) -> p c t", c=NCD)[
                        :, :, tt * TTOK:(tt + 1) * TTOK]
                p.op("dve", lambda h, tt=tt, pr=pr: h.tensor_mul(
                    nmv(tt), nmv(tt), absb[pr].ap().unsqueeze(1).broadcast_to(
                        [128, NCD, TTOK])),
                    reads=rn_tt + [Rabsb[pr]], writes=rn_tt, sem="dve", inc=1)
                p.op("dve", lambda h, tt=tt, pr=pr: h.tensor_add(
                    nmv(tt), nmv(tt), bbsb[pr].ap().unsqueeze(1).broadcast_to(
                        [128, NCD, TTOK])),
                    reads=rn_tt + [Rbbsb[pr]], writes=rn_tt, sem="dve", inc=1)
                if kind != "out":
                    for c in range(NCD):
                        wca = wcols.ap()[:, l * NCD + c: l * NCD + c + 1]
                        bca_ = bcols.ap()[:, l * NCD + c: l * NCD + c + 1]
                        p.op("dve", lambda h, c=c, tt=tt, wca=wca, bca_=bca_:
                             h.tensor_scalar(nm(c, tt), nm(c, tt), wca, bca_,
                                             ALU.mult, ALU.add),
                             reads=[RN[c][tt], Rwcols[l], Rbcols[l]],
                             writes=[RN[c][tt]], sem="dve", inc=1)
            else:
                p.op("dve", lambda h, ba=ba, pr=pr: h.reciprocal(
                    afsb[pr].ap(), pb[ba].ap()),
                    reads=[RB[ba]], writes=[Rafsb[pr]], sem="dve", inc=1)
                p.op("dve", lambda h, bb=bb, pr=pr: h.scalar_tensor_tensor(
                    bfsb[pr].ap(), pb[bb].ap(), -1.0, afsb[pr].ap(),
                    ALU.mult, ALU.mult),
                    reads=[RB[bb], Rafsb[pr]], writes=[Rbfsb[pr]],
                    sem="dve", inc=1)

                def hv(c0, c1, tt=tt):
                    return Hb.ap().rearrange("p (c t) -> p c t", c=NCD)[
                        :, c0:c1, tt * TTOK:(tt + 1) * TTOK]

                def afb(c0, c1, pr=pr):
                    return afsb[pr].ap().unsqueeze(1).broadcast_to(
                        [128, c1 - c0, TTOK])

                def bfb(c0, c1, pr=pr):
                    return bfsb[pr].ap().unsqueeze(1).broadcast_to(
                        [128, c1 - c0, TTOK])
                for (eng, sm, c0, c1) in (("dve", "dve", 0, NCD),):
                    rh_s = [RH[c][tt] for c in range(c0, c1)]
                    p.op(eng, lambda h, c0=c0, c1=c1, tt=tt, pr=pr:
                         h.tensor_mul(hv(c0, c1, tt), hv(c0, c1, tt),
                                      afb(c0, c1, pr)),
                         reads=rh_s + [Rafsb[pr]], writes=rh_s, sem=sm, inc=1)
                    p.op(eng, lambda h, c0=c0, c1=c1, tt=tt, pr=pr:
                         h.tensor_add(hv(c0, c1, tt), hv(c0, c1, tt),
                                      bfb(c0, c1, pr)),
                         reads=rh_s + [Rbfsb[pr]], writes=rh_s, sem=sm, inc=1)

        for tt in range(NT):
            stats(tt)
            rows(tt)
            if tt >= 1:
                bca(tt - 1)
        bca(NT - 1)

    soff_ctr = [0]

    def next_piece(ncols):
        so = soff_ctr[0]
        soff_ctr[0] += ncols
        return emit_piece_dma(ncols, so)

    def emit_rwkv_layer(l, seq, kvr_pieces=None, post_do=None):
        p.fence([Rkv, Rst], Rabsb + Rbbsb + Rbfsb + Rafsb)
        emit_ln("in", l, seq, lambda c, tt: H(c, tt), lambda c, tt: [RH[c][tt]])
        p.fence(Rabsb + Rbbsb + Rbfsb + Rafsb, [Rkv, Rst])
        for e in range(NCD):
            if kvr_pieces and e < len(kvr_pieces):
                tile, rw_regs = kvr_pieces[e]
            else:
                tile, rw_regs = next_piece(3 * 1024)
            for tt in range(NT):
                for wi, wname in enumerate(("k", "v", "r")):
                    bank0 = {0: 0, 1: 2, 2: 4}[wi]
                    bank = bank0 + (tt % 2)
                    pieces = []
                    for dch in range(NCD):
                        lt = tile(wi * 1024 + dch * 128, wi * 1024 + (dch + 1) * 128)
                        pieces.append((lt, nm(dch, tt), rw_regs + [RN[dch][tt]]))
                    group(bank, TTOK, pieces)
                    if wi == 0:
                        p.op("act", lambda h, tt=tt, bank=bank: h.activation(
                            out=ks_t.ap()[:, tt * TTOK:(tt + 1) * TTOK],
                            in_=pb[bank].ap(), func=AF.Copy, scale=1.0 - DEC),
                            reads=[RB[bank]], writes=[Rks], sem="act", inc=1)
                    elif wi == 1:
                        p.op("dve", lambda h, tt=tt, bank=bank: h.tensor_mul(
                            kv_t.ap()[:, tt * TTOK:(tt + 1) * TTOK],
                            ks_t.ap()[:, tt * TTOK:(tt + 1) * TTOK],
                            pb[bank].ap()),
                            reads=[Rks, RB[bank]], writes=[Rkv], sem="dve", inc=1)
                    else:
                        p.op("act", lambda h, tt=tt, bank=bank: h.activation(
                            out=sr_t.ap()[:, tt * TTOK:(tt + 1) * TTOK],
                            in_=pb[bank].ap(), func=AF.Sigmoid),
                            reads=[RB[bank]], writes=[Rsr], sem="act", inc=1)
            p.op("dve", lambda h: h.tensor_tensor_scan(
                st_t.ap(), dec_t.ap().broadcast_to([128, T]), kv_t.ap(),
                0.0, ALU.mult, ALU.add),
                reads=[Rkv, Rdec], writes=[Rst], sem="dve", inc=1)
            p.op("dve", lambda h, e=e: h.tensor_mul(yv(e), sr_t.ap(), st_t.ap()),
                 reads=[Rsr, Rst], writes=[RY[e]], sem="dve", inc=1)
        if post_do is not None:
            p.fence([Rsr, Rks], [Rxstr, Rxstrb])
        for do in range(NCD):
            tile, wo_regs = next_piece(1024)
            for tt in range(NT):
                bank = 6 + (tt % 2)
                pieces = []
                for e in range(NCD):
                    lt = tile(e * 128, (e + 1) * 128)
                    pieces.append((lt, yv(e, tt), wo_regs + [RY[e]]))
                group(bank, TTOK, pieces)
                p.op("dve", lambda h, do=do, tt=tt, bank=bank: h.tensor_add(
                    H(do, tt), H(do, tt), pb[bank].ap()),
                    reads=[RH[do][tt], RB[bank]], writes=[RH[do][tt]],
                    sem="dve", inc=1)
            if post_do is not None:
                post_do(do)

    def emit_router(seq):
        tile, rt_regs = next_piece(128)
        for tc in range(NTC):
            bank = 0 + (tc % 2)
            pieces = []
            for c in range(NCD):
                pieces.append((nm128(c, tc), tile(c * NE, (c + 1) * NE),
                               rt_regs + [RN[c][tc * 128 // TTOK]]))
            group(bank, NE, pieces)
            p.op("dve", lambda h, tc=tc, bank=bank: h.tensor_add(
                lg_all.ap()[:, tc * NE:(tc + 1) * NE], pb[bank].ap()[:, 0:NE],
                biasbc.ap()),
                reads=[RB[bank], Rbbc], writes=[Rlog], sem="dve", inc=1)
        L3 = lg_all.ap().rearrange("p (c e) -> p c e", e=NE)
        M13 = m16a.ap().unsqueeze(2).broadcast_to([128, NTC, NE])
        M23 = m16b.ap().unsqueeze(2).broadcast_to([128, NTC, NE])
        R13 = rw1_16.ap().unsqueeze(2).broadcast_to([128, NTC, NE])
        R23 = rw2_16.ap().unsqueeze(2).broadcast_to([128, NTC, NE])
        D3 = maskda.ap().rearrange("p (c e) -> p c e", e=NE)
        p.op("dve", lambda h: h.reduce_max(m16a.ap(), L3, axis=AXX),
             reads=[Rlog], writes=[Rtop], sem="dve", inc=1)
        p.op("dve", lambda h: h.tensor_tensor(mask1a.ap().rearrange(
            "p (c e) -> p c e", e=NE), L3, M13, ALU.is_ge),
            reads=[Rlog, Rtop], writes=[Rtop], sem="dve", inc=1)
        p.op("dve", lambda h: h.scalar_tensor_tensor(
            maskda.ap(), mask1a.ap(), -3.0e38, lg_all.ap(), ALU.mult, ALU.add),
            reads=[Rlog, Rtop], writes=[Rtop], sem="dve", inc=1)
        p.op("dve", lambda h: h.reduce_max(m16b.ap(), D3, axis=AXX),
             reads=[Rtop], writes=[Rtop], sem="dve", inc=1)
        p.op("dve", lambda h: h.tensor_tensor(tmpa.ap().rearrange(
            "p (c e) -> p c e", e=NE), D3, M23, ALU.is_ge),
            reads=[Rtop], writes=[Rtop], sem="dve", inc=1)
        p.op("dve", lambda h: h.tensor_sub(m16a.ap(), m16a.ap(), m16b.ap()),
             reads=[Rtop], writes=[Rtop], sem="dve", inc=1)
        p.op("act", lambda h: h.activation(out=rw1_16.ap(), in_=m16a.ap(),
                                           func=AF.Sigmoid),
             reads=[Rtop], writes=[Rtop], sem="act", inc=1)
        p.op("dve", lambda h: h.tensor_scalar(rw2_16.ap(), rw1_16.ap(), -1.0,
                                              1.0, ALU.mult, ALU.add),
             reads=[Rtop], writes=[Rtop], sem="dve", inc=1)
        p.op("dve", lambda h: h.tensor_tensor(mask1a.ap().rearrange(
            "p (c e) -> p c e", e=NE), mask1a.ap().rearrange(
            "p (c e) -> p c e", e=NE), R13, ALU.mult),
            reads=[Rtop], writes=[Rtop], sem="dve", inc=1)
        p.op("dve", lambda h: h.tensor_tensor(tmpa.ap().rearrange(
            "p (c e) -> p c e", e=NE), tmpa.ap().rearrange(
            "p (c e) -> p c e", e=NE), R23, ALU.mult),
            reads=[Rtop], writes=[Rtop], sem="dve", inc=1)
        p.op("dve", lambda h: h.tensor_add(cw_all.ap(), mask1a.ap(), tmpa.ap()),
             reads=[Rtop], writes=[Rcwall], sem="dve", inc=1)
        p.op("pe", lambda h: h.transpose(pb[2].ap()[:, 0:128], cw_all.ap(),
                                         ident.ap()),
             reads=[Rcwall, Rident], writes=[RB[2]], sem="pe", inc=1)
        p.op("act", lambda h: h.activation(out=cwT.ap(), in_=pb[2].ap()[:, 0:128],
                                           func=AF.Copy),
             reads=[RB[2]], writes=[RcwT], sem="act", inc=1)
        p.op("sp", lambda h: h.dma_start(out=scrD, in_=cwT.ap()),
             reads=[RcwT], writes=[RscrD], sem=p.dma_sem("sq"), inc=16)

    def emit_cw_prep(ex):
        src = scrD.rearrange("(c e) q -> e c q", e=NE)[ex].unsqueeze(0)
        dst = cw_row.ap().rearrange("p (c q) -> p c q", c=NTC)
        p.op("pool", lambda h, src=src, dst=dst: h.dma_start(out=dst, in_=src),
             reads=[RscrD], writes=[Rcwrow], sem=p.dma_sem("wq"), inc=16)
        for tt in range(NT):
            bank = 6 + (tt % 2)
            p.op("pe", lambda h, tt=tt, bank=bank: h.matmul(
                pb[bank].ap(), onesrowb.ap(),
                cw_row.ap()[:, tt * TTOK:(tt + 1) * TTOK], start=True, stop=True),
                reads=[Rcwrow, Ronesb], writes=[RB[bank]], sem="pe", inc=1)
            p.op("act", lambda h, tt=tt, bank=bank: h.activation(
                out=cw_bc.ap()[:, tt * TTOK:(tt + 1) * TTOK],
                in_=pb[bank].ap(), func=AF.Copy),
                reads=[RB[bank]], writes=[Rcwbc], sem="act", inc=1)

    def emit_expert(ex):
        if ex >= 0:
            emit_cw_prep(ex)
        for h1 in range(NCH):
            tile, w_regs = next_piece(2048)
            for tt in range(NT):
                bank = 0 + (tt % 2)
                pieces = []
                for dch in range(NCD):
                    lt = tile(dch * 128, (dch + 1) * 128)
                    pieces.append((lt, nm(dch, tt), w_regs + [RN[dch][tt]]))
                group(bank, TTOK, pieces)
                p.op("act", lambda h, tt=tt, bank=bank: h.activation(
                    out=silu_t[tt % 2].ap(), in_=pb[bank].ap(), func=AF.Silu),
                    reads=[RB[bank]], writes=[Rsilu[tt % 2]], sem="act", inc=1)
                banku = 2 + (tt % 2)
                pieces = []
                for dch in range(NCD):
                    lt = tile(1024 + dch * 128, 1024 + (dch + 1) * 128)
                    pieces.append((lt, nm(dch, tt), w_regs + [RN[dch][tt]]))
                group(banku, TTOK, pieces)
                p.op("dve", lambda h, h1=h1, tt=tt, banku=banku: h.tensor_mul(
                    eo(h1, tt), silu_t[tt % 2].ap(), pb[banku].ap()),
                    reads=[Rsilu[tt % 2], RB[banku]], writes=[REO[h1]],
                    sem="dve", inc=1)
        for do in range(NCD):
            tile, w_regs = next_piece(2048)
            for tt in range(NT):
                bank = 4 + (tt % 2)
                pieces = []
                for h1 in range(NCH):
                    lt = tile(h1 * 128, (h1 + 1) * 128)
                    pieces.append((lt, eo(h1, tt), w_regs + [REO[h1]]))
                group(bank, TTOK, pieces)
                if ex >= 0:
                    p.op("dve", lambda h, tt=tt, bank=bank: h.tensor_mul(
                        tmpm[tt % 2].ap(), pb[bank].ap(),
                        cw_bc.ap()[:, tt * TTOK:(tt + 1) * TTOK]),
                        reads=[RB[bank], Rcwbc], writes=[Rtmpm[tt % 2]],
                        sem="dve", inc=1)
                    p.op("dve", lambda h, do=do, tt=tt: h.tensor_add(
                        H(do, tt), H(do, tt), tmpm[tt % 2].ap()),
                        reads=[RH[do][tt], Rtmpm[tt % 2]], writes=[RH[do][tt]],
                        sem="dve", inc=1)
                else:
                    p.op("dve", lambda h, do=do, tt=tt, bank=bank: h.tensor_add(
                        H(do, tt), H(do, tt), pb[bank].ap()),
                        reads=[RH[do][tt], RB[bank]], writes=[RH[do][tt]],
                        sem="dve", inc=1)

    def emit_seq(seq):
        rwkv_regs = [RY[e] for e in range(NCD)] + [Rkv, Rst, Rsr, Rks, Rxstrb]
        row_regs = list(Rrows)
        ffn_regs = list(REO) + list(Rsilu) + list(Rtmpm) + [Rcwrow, Rcwbc]
        ab_regs = Rabsb + Rbbsb + Rbfsb + Rafsb
        p.fence(ffn_regs, rwkv_regs + row_regs + ab_regs + [Rxstr])

        l0_pieces = [next_piece(3 * 1024) for _ in range(2)]
        for c in range(NCD):
            srcx = xin[seq].rearrange("(c q) t -> c q t", c=NCD)[c]
            p.op("sp", lambda h, c=c, srcx=srcx: h.dma_start(out=H(c), in_=srcx),
                 reads=[], writes=list(RH[c]), sem=p.dma_sem("sq"), inc=16)
        emit_ln("ln1", 0, seq, lambda c, tt: H(c, tt), lambda c, tt: [RH[c][tt]])
        if seq == 0:
            tap("h0", Hb.ap(), sum(RH, []))

        def final_add(do):
            buf = (xstr, Rxstr) if do % 2 == 0 else (xstr_b, Rxstrb)
            srcx = xin[seq].rearrange("(c q) t -> c q t", c=NCD)[do]
            p.op("sp", lambda h, srcx=srcx, buf=buf: h.dma_start(
                out=buf[0].ap(), in_=srcx),
                reads=[], writes=[buf[1]], sem=p.dma_sem("sq"), inc=16)
            p.op("dve", lambda h, do=do, buf=buf: h.tensor_add(
                H(do), H(do), buf[0].ap()),
                reads=RH[do] + [buf[1]], writes=list(RH[do]), sem="dve", inc=1)

        for l in range(NLAY):
            emit_rwkv_layer(l, seq, kvr_pieces=l0_pieces if l == 0 else None,
                            post_do=final_add if l == NLAY - 1 else None)
            if seq == 0:
                tap("hl%d" % l, Hb.ap(), sum(RH, []))
                if l == 0:
                    tap("nm0", NM.ap(), sum(RN, []))
                    tap("y0", y_t.ap(), list(RY))
                    tap("ks7", ks_t.ap(), [Rks])
                    tap("kv7", kv_t.ap(), [Rkv])
                    tap("st7", st_t.ap(), [Rst])
                    tap("sr7", sr_t.ap(), [Rsr])
        if seq == 0:
            tap("x1", Hb.ap(), sum(RH, []))
        p.fence([Rkv, Rst], Rabsb + Rbbsb + Rbfsb + Rafsb)
        emit_ln("out", 0, seq, lambda c, tt: H(c, tt), lambda c, tt: [RH[c][tt]])
        if seq == 0:
            tap("nm2", NM.ap(), sum(RN, []))
        emit_router(seq)
        if seq == 0:
            tap("cwa", cw_all.ap(), [Rcwall])
        p.fence(rwkv_regs + row_regs + ab_regs + [Rxstr], ffn_regs)
        for ex in range(-1, NE):
            emit_expert(ex)
            if seq == 0 and ex == -1:
                tap("hsh", Hb.ap(), sum(RH, []))
        for c in range(NCD):
            dst = xout[seq].rearrange("(c q) t -> c q t", c=NCD)[c]
            p.op("sp", lambda h, c=c, dst=dst: h.dma_start(out=dst, in_=H(c)),
                 reads=list(RH[c]), writes=[], sem=p.dma_sem("sq"), inc=16)

    init()
    for seq in range(BL):
        soff_ctr[0] = 0
        emit_seq(seq)
    p.wait("sp", {f"sq{i}": p.cnt[f"sq{i}"] for i in range(NSQ)}, force=True)

    def runner(qname):
        def f(h):
            for emit, waits, mark in p.q[qname]:
                for s, c in waits:
                    h.wait_ge(sems[s], c)
                if emit is not None:
                    inst = emit(h)
                    if mark is not None:
                        inst.then_inc(sems[mark[0]], mark[2])
        return f

    with nc.Block() as blk:
        blk.sync(runner("sp"))
        blk.gpsimd(runner("pool"))
        blk.tensor(runner("pe"))
        blk.vector(runner("dve"))
        blk.scalar(runner("act"))

    es.close()
    return nc


_NC_CACHE = {}


def kernel(**inputs):
    x = np.ascontiguousarray(np.asarray(inputs["L0_attn_h"], np.float32))
    ws = pack_stream(inputs)
    lnw = np.asarray(inputs["L0_attn_ln_w"], np.float32)
    lnb = np.asarray(inputs["L0_attn_ln_b"], np.float32)
    bias = np.asarray(inputs["L0_ffn_experts_bias"], np.float32)
    lnw_cols = np.ascontiguousarray(lnw.reshape(NLAY, NCD, 128).transpose(0, 2, 1))
    lnb_cols = np.ascontiguousarray(lnb.reshape(NLAY, NCD, 128).transpose(0, 2, 1))
    ident = np.eye(128, dtype=np.float32)
    bias_row = np.ascontiguousarray(bias.reshape(1, NE))

    xbdt = np.ascontiguousarray(x.transpose(0, 2, 1))

    key = "nc_dbg" if DEBUG_TAPS else "nc"
    if key not in _NC_CACHE:
        _NC_CACHE[key] = build_nc(debug=DEBUG_TAPS)
    nc = _NC_CACHE[key]

    in_maps = []
    for c in range(NCORE):
        in_maps.append({
            "xin": np.ascontiguousarray(xbdt[c * BL:(c + 1) * BL]),
            "wstream": ws,
            "ident": ident,
            "lnwcols": lnw_cols,
            "lnbcols": lnb_cols,
            "biasrow": bias_row,
        })
    res = run_bass_kernel_spmd(nc, in_maps, core_ids=list(range(NCORE)))
    out = np.empty((B, T, D), np.float32)
    for c in range(NCORE):
        o = np.asarray(res.results[c]["xout"], np.float32)
        out[c * BL:(c + 1) * BL] = o.transpose(0, 2, 1)
    if DEBUG_TAPS:
        kernel._last_taps = [res.results[c] for c in range(NCORE)]
    return out


# version 22
# speedup vs baseline: 1.0497x; 1.0497x over previous
import numpy as np
import ml_dtypes
import concourse.bass as bass
import concourse.mybir as mybir
from concourse.bass_utils import run_bass_kernel_spmd
from contextlib import ExitStack

F32 = mybir.dt.float32
BF16 = mybir.dt.bfloat16
U8 = mybir.dt.uint8
AF = mybir.ActivationFunctionType
ALU = mybir.AluOpType
AXX = mybir.AxisListType.X
BFNP = ml_dtypes.bfloat16

B, T, D, HD, NE, NLAY = 16, 2048, 1024, 2048, 8, 2
NCORE = 8
BL = B // NCORE
NCD = D // 128
NCH = HD // 128
TTOK = 512
NT = T // TTOK
NTC = T // 128
DEC = 0.95
EPS = 1e-5
INV_D = 1.0 / D
RING = 8448
RSLOT = 64

ENGS = ("pe", "dve", "act", "pool", "sp")
NWQ, NSQ = 16, 8
SEMS = ("pe", "dve", "act", "pl") + tuple(f"wq{i}" for i in range(NWQ)) + \
    tuple(f"sq{i}" for i in range(NSQ))


def make_plan():
    p = []
    for l in range(NLAY):
        for e in range(NCD):
            p.append(("kvr", l, e, 3 * 1024))
        for do in range(NCD):
            p.append(("wo", l, do, 1024))
    p.append(("router", 0, 0, 128))
    for ex in range(-1, NE):
        for h1 in range(NCH):
            p.append(("w13", ex, h1, 2048))
        for do in range(NCD):
            p.append(("w2", ex, do, 2048))
    return p


def _tile128(mat):
    K, C = mat.shape
    n = K // 128
    return mat.reshape(n, 128, C).transpose(1, 0, 2).reshape(128, n * C)


def pack_stream(ins):
    Wk, Wv, Wr, Wo = (np.asarray(ins[k], np.float32) for k in
                      ("L0_attn_Wk", "L0_attn_Wv", "L0_attn_Wr", "L0_attn_Wo"))
    rw = np.asarray(ins["L0_ffn_router_w"], np.float32)
    w1 = np.asarray(ins["L0_ffn_w1"], np.float32)
    w3 = np.asarray(ins["L0_ffn_w3"], np.float32)
    w2 = np.asarray(ins["L0_ffn_w2"], np.float32)
    sw1 = np.asarray(ins["L0_ffn_sw1"], np.float32)
    sw3 = np.asarray(ins["L0_ffn_sw3"], np.float32)
    sw2 = np.asarray(ins["L0_ffn_sw2"], np.float32)
    WkT = [Wk[l].T.copy() for l in range(NLAY)]
    WvT = [Wv[l].T.copy() for l in range(NLAY)]
    WrT = [Wr[l].T.copy() for l in range(NLAY)]
    WoT = [Wo[l].T.copy() for l in range(NLAY)]
    RT = rw.T.copy()
    cols = []
    for kind, a, b, n in make_plan():
        if kind == "kvr":
            l, e = a, b
            sl = slice(e * 128, (e + 1) * 128)
            cols.append(np.concatenate(
                [_tile128(WkT[l][:, sl]), _tile128(WvT[l][:, sl]),
                 _tile128(WrT[l][:, sl])], axis=1))
        elif kind == "wo":
            l, do = a, b
            cols.append(_tile128(WoT[l][:, do * 128:(do + 1) * 128]))
        elif kind == "router":
            rt = np.zeros((128, 128), np.float32)
            rt[:, :64] = _tile128(RT)
            cols.append(rt)
        elif kind == "w13":
            ex, h1 = a, b
            m1 = sw1[0] if ex < 0 else w1[ex]
            m3 = sw3[0] if ex < 0 else w3[ex]
            sl = slice(h1 * 128, (h1 + 1) * 128)
            cols.append(np.concatenate(
                [_tile128(m1[:, sl]), _tile128(m3[:, sl])], axis=1))
        elif kind == "w2":
            ex, do = a, b
            m2 = sw2[0] if ex < 0 else w2[ex]
            cols.append(_tile128(m2[:, do * 128:(do + 1) * 128]))
    ws = np.concatenate(cols, axis=1)
    return np.ascontiguousarray(ws)


class Reg:
    __slots__ = ("w", "r")

    def __init__(self):
        self.w = None
        self.r = {}


class Prog:
    def __init__(self):
        self.q = {e: [] for e in ENGS}
        self.cnt = {s: 0 for s in SEMS}
        self.waited = {e: {} for e in ENGS}
        self._rot = {"wq": 0, "sq": 0}

    def dma_sem(self, queue):
        n = {"wq": NWQ, "sq": NSQ}[queue]
        i = self._rot[queue]
        self._rot[queue] = (i + 1) % n
        s = f"{queue}{i}"
        if self.cnt[s] > 0:
            eng = {"wq": "pool", "sq": "sp"}[queue]
            self.wait(eng, {s: self.cnt[s]}, force=True)
        return s

    def op(self, eng, emit, reads=(), writes=(), sem=None, inc=0, rmark=None,
           war=()):
        deps = {}
        for r in reads:
            if r.w:
                deps[r.w[0]] = max(deps.get(r.w[0], 0), r.w[1])
        for w in writes:
            if w.w:
                deps[w.w[0]] = max(deps.get(w.w[0], 0), w.w[1])
            for s, c in w.r.items():
                deps[s] = max(deps.get(s, 0), c)
        for w in war:
            if w.w:
                deps[w.w[0]] = max(deps.get(w.w[0], 0), w.w[1])
            for s, c in w.r.items():
                deps[s] = max(deps.get(s, 0), c)
        own = {"pe": "pe", "sp": "sq", "pool": "wq"}.get(eng)
        if own is not None:
            for s in [s for s in deps if s.startswith(own) or s == own]:
                deps.pop(s)
        wt = self.waited[eng]
        waits = []
        for s in sorted(deps):
            c = deps[s]
            if c > 0 and wt.get(s, 0) < c:
                waits.append((s, c))
                wt[s] = c
        if sem is not None and inc:
            self.cnt[sem] += inc
            mark = (sem, self.cnt[sem], inc)
        else:
            mark = None
        self.q[eng].append((emit, waits, mark))
        rm = (mark[0], mark[1]) if mark else rmark
        if rm:
            for r in reads:
                r.r[rm[0]] = max(r.r.get(rm[0], 0), rm[1])
            for w in writes:
                w.w = rm
                w.r = {}
        return mark

    def wait(self, eng, deps, force=False):
        own = {"pe": "pe", "sp": "sq", "pool": "wq"}.get(eng)
        if own is not None and not force:
            for s in [s for s in deps if s.startswith(own) or s == own]:
                deps.pop(s)
        wt = self.waited[eng]
        waits = []
        for s in sorted(deps):
            c = deps[s]
            if c > 0 and wt.get(s, 0) < c:
                waits.append((s, c))
                wt[s] = c
        if waits:
            self.q[eng].append((None, waits, None))

    def fence(self, src_regs, dst_regs):
        deps = {}
        for r in src_regs:
            if r.w:
                deps[r.w[0]] = max(deps.get(r.w[0], 0), r.w[1])
            for s, c in r.r.items():
                deps[s] = max(deps.get(s, 0), c)
        for d in dst_regs:
            for s, c in deps.items():
                d.r[s] = max(d.r.get(s, 0), c)


DEBUG_TAPS = False

def build_nc(debug=False):
    nc = bass.Bass("TRN2", target_bir_lowering=False)
    plan = make_plan()
    WTOT = sum(it[3] for it in plan)
    taps = {}
    if debug:
        for nmx, shp, dt in (("ks7", [128, T], BF16), ("kv7", [128, T], F32),
                             ("st7", [128, T], BF16), ("sr7", [128, T], BF16),
                             ("h0", [128, NCD * T], F32), ("hl0", [128, NCD * T], F32),
                             ("hl1", [128, NCD * T], F32), ("x1", [128, NCD * T], F32),
                             ("nm0", [128, NCD * T], BF16), ("nm2", [128, NCD * T], BF16),
                             ("cwa", [128, NTC * NE], F32), ("hsh", [128, NCD * T], F32),
                             ("y0", [128, NCD * T], BF16)):
            taps[nmx] = nc.dram_tensor("tap_" + nmx, shp, dt, kind="ExternalOutput").ap()

    xin = nc.dram_tensor("xin", [BL, D, T], F32, kind="ExternalInput").ap()
    ws = nc.dram_tensor("wstream", [128, WTOT], F32, kind="ExternalInput").ap()
    ident_in = nc.dram_tensor("ident", [128, 128], F32, kind="ExternalInput").ap()
    lnw_in = nc.dram_tensor("lnwcols", [NLAY, 128, NCD], F32, kind="ExternalInput").ap()
    lnb_in = nc.dram_tensor("lnbcols", [NLAY, 128, NCD], F32, kind="ExternalInput").ap()
    bias_in = nc.dram_tensor("biasrow", [1, NE], F32, kind="ExternalInput").ap()
    xout = nc.dram_tensor("xout", [BL, D, T], F32, kind="ExternalOutput").ap()
    scrD = nc.dram_tensor("scrD", [128, 128], F32, kind="Internal").ap()

    es = ExitStack()
    Hb = nc.alloc_sbuf_tensor("Hb", [128, NCD * T], F32)
    NM = nc.alloc_sbuf_tensor("NM", [128, NCD * T], BF16)
    ring = nc.alloc_sbuf_tensor("ring", [128, RING], BF16)
    ident = nc.alloc_sbuf_tensor("ident_s", [128, 128], F32)
    cw_all = nc.alloc_sbuf_tensor("cw_all", [128, NTC * NE], F32)
    cwT = nc.alloc_sbuf_tensor("cwT", [128, 128], F32)
    onesrow = nc.alloc_sbuf_tensor("onesrow", [1, 128], F32)
    onesrowb = nc.alloc_sbuf_tensor("onesrowb", [1, 128], BF16)
    onescolb = nc.alloc_sbuf_tensor("onescolb", [128, 1], BF16)
    dec_t = nc.alloc_sbuf_tensor("dec_t", [128, 1], F32)
    epsc = nc.alloc_sbuf_tensor("epsc", [128, 1], F32)
    nc.const_aps.register(F32, EPS, epsc.ap())
    wcols = nc.alloc_sbuf_tensor("wcols", [128, NLAY * NCD], F32)
    bcols = nc.alloc_sbuf_tensor("bcols", [128, NLAY * NCD], F32)
    biasrow = nc.alloc_sbuf_tensor("biasrow_s", [1, NE], F32)
    biasbc = nc.alloc_sbuf_tensor("biasbc", [128, NE], F32)
    lg_all = nc.alloc_sbuf_tensor("lg_all", [128, NTC * NE], F32)
    m16a = nc.alloc_sbuf_tensor("m16a", [128, NTC], F32)
    m16b = nc.alloc_sbuf_tensor("m16b", [128, NTC], F32)
    rw1_16 = nc.alloc_sbuf_tensor("rw1_16", [128, NTC], F32)
    rw2_16 = nc.alloc_sbuf_tensor("rw2_16", [128, NTC], F32)
    mask1a = nc.alloc_sbuf_tensor("mask1a", [128, NTC * NE], F32)
    maskda = nc.alloc_sbuf_tensor("maskda", [128, NTC * NE], F32)
    tmpa = nc.alloc_sbuf_tensor("tmpa", [128, NTC * NE], F32)
    xb = nc.alloc_sbuf_tensor("xb", [128, 2 * TTOK], BF16)
    x2b = nc.alloc_sbuf_tensor("x2b", [128, 2 * TTOK], BF16)
    t1f = nc.alloc_sbuf_tensor("t1f", [128, TTOK], F32)
    t2f = nc.alloc_sbuf_tensor("t2f", [128, TTOK], F32)

    arena = nc.alloc_sbuf_tensor("arena", [128, 81920], U8)
    base = nc.lookup_mloc(arena).addr
    at = nc.alloc_sbuf_tensor_at
    eo_t = at("eo", [128, NCH * T], BF16, offset=base + 0)
    y_t = at("yt", [128, NCD * T], BF16, offset=base + 0)
    kv_t = at("kvt", [128, T], F32, offset=base + 32768)
    absb = [at(f"absb{i}", [128, TTOK], BF16, offset=base + 32768 + i * 1024)
            for i in range(2)]
    bbsb = [at(f"bbsb{i}", [128, TTOK], BF16, offset=base + 34816 + i * 1024)
            for i in range(2)]
    bfsb = [at(f"bfsb{i}", [128, TTOK], F32, offset=base + 36864 + i * 2048)
            for i in range(2)]
    afsb = [at(f"afsb{i}", [128, TTOK], F32, offset=base + 40960 + i * 2048)
            for i in range(2)]
    st_t = at("stt", [128, T], BF16, offset=base + 40960)
    sr_t = at("srt", [128, T], BF16, offset=base + 45056)
    ks_t = at("kst", [128, T], BF16, offset=base + 49152)
    r_sum = at("r_sum", [1, TTOK], F32, offset=base + 53248)
    r_sq = at("r_sq", [1, TTOK], F32, offset=base + 55296)
    r_t = at("r_t", [1, TTOK], F32, offset=base + 57344)
    r_rstd2 = [at(f"r_rstd{i}", [1, TTOK], F32, offset=base + 59392 + i * 2048)
               for i in range(2)]
    r_nmr2 = [at(f"r_nmr{i}", [1, TTOK], F32, offset=base + 63488 + i * 2048)
              for i in range(2)]
    xstr = at("xstr", [128, T], F32, offset=base + 67584)
    xstr_b = at("xstr_b", [128, T], F32, offset=base + 45056)
    silu_t = [at(f"silu{i}", [128, TTOK], BF16, offset=base + 67584 + i * 1024)
              for i in range(2)]
    tmpm = [at(f"tmpm{i}", [128, TTOK], F32, offset=base + 69632 + i * 2048)
            for i in range(2)]
    cw_row = at("cw_row", [1, T], BF16, offset=base + 73728)
    cw_bc = at("cw_bc", [128, T], BF16, offset=base + 77824)

    pb = [nc.alloc_psum_tensor(f"pb{i}", [128, 512], F32) for i in range(8)]

    sems = {}
    for s in SEMS:
        sems[s] = es.enter_context(nc.semaphore(s))

    p = Prog()

    RH = [[Reg() for _ in range(NT)] for _ in range(NCD)]
    RN = [[Reg() for _ in range(NT)] for _ in range(NCD)]
    RY = [Reg() for _ in range(NCD)]
    REO = [Reg() for _ in range(NCH)]
    RB = [Reg() for _ in range(8)]
    Rkv, Rst, Rsr, Rks = Reg(), Reg(), Reg(), Reg()
    Rrows = [Reg() for _ in range(3)]
    Rrstd2 = [Reg(), Reg()]
    Rnmr2 = [Reg(), Reg()]
    Rxstr, Rxstrb, Rt1, Rt2 = Reg(), Reg(), Reg(), Reg()
    Rxb = [Reg(), Reg()]
    Rx2b = [Reg(), Reg()]
    Rabsb = [Reg(), Reg()]
    Rbbsb = [Reg(), Reg()]
    Rbfsb = [Reg(), Reg()]
    Rafsb = [Reg(), Reg()]
    Rsilu = [Reg(), Reg()]
    Rtmpm = [Reg(), Reg()]
    Rcwrow, Rcwbc, Rcwall, RcwT, RscrD = Reg(), Reg(), Reg(), Reg(), Reg()
    Rdec, Reps, Rones, Ronesb, Ronescol = Reg(), Reg(), Reg(), Reg(), Reg()
    Rident, Rbrow, Rbbc = Reg(), Reg(), Reg()
    Rwcols = [Reg() for _ in range(NLAY)]
    Rbcols = [Reg() for _ in range(NLAY)]
    Rtop = Reg()
    Rlog = Reg()

    nslot = RING // RSLOT
    ring_slot_free = [Reg() for _ in range(nslot)]

    def H(c, tt=None):
        a = Hb.ap()
        return a[:, c * T:(c + 1) * T] if tt is None else \
            a[:, c * T + tt * TTOK: c * T + (tt + 1) * TTOK]

    def nm(c, tt=None):
        a = NM.ap()
        if tt is None:
            return a[:, c * T:(c + 1) * T]
        return a[:, c * T + tt * TTOK: c * T + (tt + 1) * TTOK]

    def nm128(c, tc):
        return NM.ap()[:, c * T + tc * 128: c * T + tc * 128 + 128]

    def eo(h1, tt):
        return eo_t.ap()[:, h1 * T + tt * TTOK: h1 * T + (tt + 1) * TTOK]

    def yv(e, tt=None):
        a = y_t.ap()
        if tt is None:
            return a[:, e * T:(e + 1) * T]
        return a[:, e * T + tt * TTOK: e * T + (tt + 1) * TTOK]

    ring_pos = [0]

    def ring_alloc(ncols, soff):
        segs = []
        left = ncols
        cur = soff
        while left > 0:
            pos = ring_pos[0]
            n = min(left, RING - pos)
            segs.append((pos, n, cur))
            ring_pos[0] = (pos + n) % RING
            left -= n
            cur += n
        return segs

    def emit_piece_dma(ncols, soff):
        segs = ring_alloc(ncols, soff)
        regs = []
        for (roff, n, so) in segs:
            sreg = Reg()
            s0 = roff // RSLOT
            s1 = (roff + n - 1) // RSLOT
            prev, seen = [], set()
            for i in range(s0, s1 + 1):
                r0 = ring_slot_free[i]
                if r0 is not None and id(r0) not in seen:
                    seen.add(id(r0))
                    prev.append(r0)
                    assert r0.r or r0.w is None, (
                        "ring WAR hazard: overwriting unconsumed piece")

            def emit(h, roff=roff, n=n, so=so):
                return h.dma_start(out=ring.ap()[:, roff:roff + n],
                                   in_=ws[:, so:so + n])
            p.op("pool", emit, reads=[], writes=[sreg], war=prev,
                 sem=p.dma_sem("wq"), inc=16)
            for i in range(s0, s1 + 1):
                ring_slot_free[i] = sreg
            regs.append(sreg)
        base_off = segs[0][0]
        contig = len(segs) == 1

        def tile(a, b):
            if contig:
                return ring.ap()[:, base_off + a: base_off + b]
            acc = 0
            for (roff, n, _so) in segs:
                if a >= acc and b <= acc + n:
                    return ring.ap()[:, roff + (a - acc): roff + (b - acc)]
                acc += n
            raise AssertionError("ring tile spans wrap; adjust RSLOT/RING")
        return tile, regs

    def group(bank, outw, pieces):
        n = len(pieces)
        endmark = ("pe", p.cnt["pe"] + 1)
        for i, (lt, rh, regs) in enumerate(pieces):
            first, last = i == 0, i == n - 1
            m = lt.shape[-1]

            def emit(h, lt=lt, rh=rh, first=first, last=last, bank=bank, m=m, outw=outw):
                return h.matmul(pb[bank].ap()[0:m, 0:outw], lt, rh,
                                start=first, stop=last)
            if last:
                p.op("pe", emit, reads=list(regs), writes=[RB[bank]], sem="pe", inc=1)
            else:
                p.op("pe", emit, reads=list(regs),
                     writes=[RB[bank]] if first else [], rmark=endmark)
        return endmark

    def tap(name, src_t, regs):
        if not debug or name not in taps:
            return
        p.op("sp", lambda h: h.dma_start(out=taps[name], in_=src_t),
             reads=regs, writes=[], sem=p.dma_sem("sq"), inc=16)

    def init():
        p.op("dve", lambda h: h.memset(dec_t.ap(), DEC), writes=[Rdec], sem="dve", inc=1)
        p.op("dve", lambda h: h.memset(epsc.ap(), EPS), writes=[Reps], sem="dve", inc=1)
        p.op("dve", lambda h: h.memset(onesrow.ap(), 1.0), writes=[Rones], sem="dve", inc=1)
        p.op("dve", lambda h: h.memset(onesrowb.ap(), 1.0), writes=[Ronesb], sem="dve", inc=1)
        p.op("dve", lambda h: h.memset(onescolb.ap(), 1.0), writes=[Ronescol], sem="dve", inc=1)
        p.op("sp", lambda h: h.dma_start(out=ident.ap(), in_=ident_in),
             writes=[Rident], sem=p.dma_sem("sq"), inc=16)
        for l in range(NLAY):
            p.op("sp", lambda h, l=l: h.dma_start(
                out=wcols.ap()[:, l * NCD:(l + 1) * NCD], in_=lnw_in[l]),
                writes=[Rwcols[l]], sem=p.dma_sem("sq"), inc=16)
            p.op("sp", lambda h, l=l: h.dma_start(
                out=bcols.ap()[:, l * NCD:(l + 1) * NCD], in_=lnb_in[l]),
                writes=[Rbcols[l]], sem=p.dma_sem("sq"), inc=16)
        p.op("sp", lambda h: h.dma_start(out=biasrow.ap(), in_=bias_in),
             writes=[Rbrow], sem=p.dma_sem("sq"), inc=16)
        p.op("pe", lambda h: h.matmul(pb[0].ap()[:, 0:NE], onesrow.ap(),
                                      biasrow.ap(), start=True, stop=True),
             reads=[Rones, Rbrow], writes=[RB[0]], sem="pe", inc=1)
        p.op("act", lambda h: h.activation(out=biasbc.ap(), in_=pb[0].ap()[:, 0:NE],
                                           func=AF.Copy),
             reads=[RB[0]], writes=[Rbbc], sem="act", inc=1)

    def emit_ln(kind, l, seq, src, src_regs_fn):
        nmout = kind != "ln1"

        def stats(tt):
            bs, bq = 0 + (tt % 2), 2 + (tt % 2)
            for c in range(NCD):
                sl = (c % 2)
                x2a = x2b.ap()[:, sl * TTOK:(sl + 1) * TTOK]
                sregs = src_regs_fn(c, tt)
                if nmout:
                    cast_dst, cast_reg = nm(c, tt), RN[c][tt]
                else:
                    cast_dst = xb.ap()[:, sl * TTOK:(sl + 1) * TTOK]
                    cast_reg = Rxb[sl]
                p.op("act", lambda h, c=c, tt=tt, cast_dst=cast_dst:
                     h.activation(out=cast_dst, in_=src(c, tt), func=AF.Copy),
                     reads=sregs, writes=[cast_reg], sem="act", inc=1)
                p.op("dve", lambda h, x2a=x2a, cast_dst=cast_dst: h.tensor_mul(
                    x2a, cast_dst, cast_dst),
                    reads=[cast_reg], writes=[Rx2b[sl]], sem="dve", inc=1)
                first, last = c == 0, c == NCD - 1
                p.op("pe", lambda h, first=first, last=last, bs=bs,
                     cast_dst=cast_dst: h.matmul(
                         pb[bs].ap()[0:1, :], onescolb.ap(), cast_dst,
                         start=first, stop=last),
                     reads=[cast_reg, Ronescol],
                     writes=[RB[bs]] if (first or last) else [],
                     sem="pe", inc=1)
                p.op("pe", lambda h, first=first, last=last, bq=bq, x2a=x2a:
                     h.matmul(pb[bq].ap()[0:1, :], onescolb.ap(), x2a,
                              start=first, stop=last),
                     reads=[Rx2b[sl], Ronescol],
                     writes=[RB[bq]] if (first or last) else [],
                     sem="pe", inc=1)

        def rows(tt):
            bs, bq, pr = 0 + (tt % 2), 2 + (tt % 2), tt % 2
            rstd_a, nmr_a = r_rstd2[pr].ap(), r_nmr2[pr].ap()
            p.op("act", lambda h, bs=bs: h.activation(
                out=r_sum.ap(), in_=pb[bs].ap()[0:1, :], func=AF.Copy,
                scale=INV_D),
                reads=[RB[bs]], writes=[Rrows[0]], sem="act", inc=1)
            p.op("act", lambda h, bq=bq: h.activation(
                out=r_sq.ap(), in_=pb[bq].ap()[0:1, :], func=AF.Copy,
                scale=INV_D),
                reads=[RB[bq]], writes=[Rrows[1]], sem="act", inc=1)
            p.op("dve", lambda h: h.tensor_mul(r_t.ap(), r_sum.ap(), r_sum.ap()),
                 reads=[Rrows[0]], writes=[Rrows[2]], sem="dve", inc=1)
            p.op("dve", lambda h: h.tensor_sub(r_t.ap(), r_sq.ap(), r_t.ap()),
                 reads=[Rrows[1], Rrows[2]], writes=[Rrows[2]], sem="dve", inc=1)
            p.op("act", lambda h: h.activation(out=r_t.ap(), in_=r_t.ap(),
                                               func=AF.Ln, bias=EPS),
                 reads=[Rrows[2], Reps], writes=[Rrows[2]], sem="act", inc=1)
            p.op("act", lambda h, rstd_a=rstd_a: h.activation(
                out=rstd_a, in_=r_t.ap(), func=AF.Exp, scale=-0.5),
                reads=[Rrows[2]], writes=[Rrstd2[pr]], sem="act", inc=1)
            p.op("dve", lambda h, rstd_a=rstd_a, nmr_a=nmr_a:
                 h.scalar_tensor_tensor(nmr_a, r_sum.ap(), -1.0, rstd_a,
                                        ALU.mult, ALU.mult),
                 reads=[Rrows[0], Rrstd2[pr]], writes=[Rnmr2[pr]],
                 sem="dve", inc=1)

        def bca(tt):
            ba, bb, pr = 4 + (tt % 2), 6 + (tt % 2), tt % 2
            rstd_a, nmr_a = r_rstd2[pr].ap(), r_nmr2[pr].ap()
            p.op("pe", lambda h, ba=ba, rstd_a=rstd_a: h.matmul(
                pb[ba].ap(), onesrow.ap(), rstd_a, start=True, stop=True),
                reads=[Rrstd2[pr], Rones], writes=[RB[ba]], sem="pe", inc=1)
            p.op("pe", lambda h, bb=bb, nmr_a=nmr_a: h.matmul(
                pb[bb].ap(), onesrow.ap(), nmr_a, start=True, stop=True),
                reads=[Rnmr2[pr], Rones], writes=[RB[bb]], sem="pe", inc=1)
            if nmout:
                p.op("act", lambda h, ba=ba, pr=pr: h.activation(
                    out=absb[pr].ap(), in_=pb[ba].ap(), func=AF.Copy),
                    reads=[RB[ba]], writes=[Rabsb[pr]], sem="act", inc=1)
                p.op("act", lambda h, bb=bb, pr=pr: h.activation(
                    out=bbsb[pr].ap(), in_=pb[bb].ap(), func=AF.Copy),
                    reads=[RB[bb]], writes=[Rbbsb[pr]], sem="act", inc=1)
                rn_tt = [RN[c][tt] for c in range(NCD)]

                def nmv(tt=tt):
                    return NM.ap().rearrange("p (c t) -> p c t", c=NCD)[
                        :, :, tt * TTOK:(tt + 1) * TTOK]
                p.op("dve", lambda h, tt=tt, pr=pr: h.tensor_mul(
                    nmv(tt), nmv(tt), absb[pr].ap().unsqueeze(1).broadcast_to(
                        [128, NCD, TTOK])),
                    reads=rn_tt + [Rabsb[pr]], writes=rn_tt, sem="dve", inc=1)
                p.op("dve", lambda h, tt=tt, pr=pr: h.tensor_add(
                    nmv(tt), nmv(tt), bbsb[pr].ap().unsqueeze(1).broadcast_to(
                        [128, NCD, TTOK])),
                    reads=rn_tt + [Rbbsb[pr]], writes=rn_tt, sem="dve", inc=1)
                if kind != "out":
                    for c in range(NCD):
                        wca = wcols.ap()[:, l * NCD + c: l * NCD + c + 1]
                        bca_ = bcols.ap()[:, l * NCD + c: l * NCD + c + 1]
                        p.op("dve", lambda h, c=c, tt=tt, wca=wca, bca_=bca_:
                             h.tensor_scalar(nm(c, tt), nm(c, tt), wca, bca_,
                                             ALU.mult, ALU.add),
                             reads=[RN[c][tt], Rwcols[l], Rbcols[l]],
                             writes=[RN[c][tt]], sem="dve", inc=1)
            else:
                p.op("act", lambda h, ba=ba, pr=pr: h.activation(
                    out=afsb[pr].ap(), in_=pb[ba].ap(), func=AF.Copy),
                    reads=[RB[ba]], writes=[Rafsb[pr]], sem="act", inc=1)
                p.op("act", lambda h, bb=bb, pr=pr: h.activation(
                    out=bfsb[pr].ap(), in_=pb[bb].ap(), func=AF.Copy),
                    reads=[RB[bb]], writes=[Rbfsb[pr]], sem="act", inc=1)

                def hv(c0, c1, tt=tt):
                    return Hb.ap().rearrange("p (c t) -> p c t", c=NCD)[
                        :, c0:c1, tt * TTOK:(tt + 1) * TTOK]

                def afb(c0, c1, pr=pr):
                    return afsb[pr].ap().unsqueeze(1).broadcast_to(
                        [128, c1 - c0, TTOK])

                def bfb(c0, c1, pr=pr):
                    return bfsb[pr].ap().unsqueeze(1).broadcast_to(
                        [128, c1 - c0, TTOK])
                for (eng, sm, c0, c1) in (("dve", "dve", 0, NCD),):
                    rh_s = [RH[c][tt] for c in range(c0, c1)]
                    p.op(eng, lambda h, c0=c0, c1=c1, tt=tt, pr=pr:
                         h.tensor_mul(hv(c0, c1, tt), hv(c0, c1, tt),
                                      afb(c0, c1, pr)),
                         reads=rh_s + [Rafsb[pr]], writes=rh_s, sem=sm, inc=1)
                    p.op(eng, lambda h, c0=c0, c1=c1, tt=tt, pr=pr:
                         h.tensor_add(hv(c0, c1, tt), hv(c0, c1, tt),
                                      bfb(c0, c1, pr)),
                         reads=rh_s + [Rbfsb[pr]], writes=rh_s, sem=sm, inc=1)

        for tt in range(NT):
            stats(tt)
            rows(tt)
            if tt >= 1:
                bca(tt - 1)
        bca(NT - 1)

    soff_ctr = [0]

    def next_piece(ncols):
        so = soff_ctr[0]
        soff_ctr[0] += ncols
        return emit_piece_dma(ncols, so)

    def emit_rwkv_layer(l, seq, kvr_pieces=None, post_do=None):
        p.fence([Rkv, Rst], Rabsb + Rbbsb + Rbfsb + Rafsb)
        emit_ln("in", l, seq, lambda c, tt: H(c, tt), lambda c, tt: [RH[c][tt]])
        p.fence(Rabsb + Rbbsb + Rbfsb + Rafsb, [Rkv, Rst])
        for e in range(NCD):
            if kvr_pieces and e < len(kvr_pieces):
                tile, rw_regs = kvr_pieces[e]
            else:
                tile, rw_regs = next_piece(3 * 1024)
            for tt in range(NT):
                for wi, wname in enumerate(("k", "v", "r")):
                    bank0 = {0: 0, 1: 2, 2: 4}[wi]
                    bank = bank0 + (tt % 2)
                    pieces = []
                    for dch in range(NCD):
                        lt = tile(wi * 1024 + dch * 128, wi * 1024 + (dch + 1) * 128)
                        pieces.append((lt, nm(dch, tt), rw_regs + [RN[dch][tt]]))
                    group(bank, TTOK, pieces)
                    if wi == 0:
                        p.op("act", lambda h, tt=tt, bank=bank: h.activation(
                            out=ks_t.ap()[:, tt * TTOK:(tt + 1) * TTOK],
                            in_=pb[bank].ap(), func=AF.Copy, scale=1.0 - DEC),
                            reads=[RB[bank]], writes=[Rks], sem="act", inc=1)
                    elif wi == 1:
                        p.op("dve", lambda h, tt=tt, bank=bank: h.tensor_mul(
                            kv_t.ap()[:, tt * TTOK:(tt + 1) * TTOK],
                            ks_t.ap()[:, tt * TTOK:(tt + 1) * TTOK],
                            pb[bank].ap()),
                            reads=[Rks, RB[bank]], writes=[Rkv], sem="dve", inc=1)
                    else:
                        p.op("act", lambda h, tt=tt, bank=bank: h.activation(
                            out=sr_t.ap()[:, tt * TTOK:(tt + 1) * TTOK],
                            in_=pb[bank].ap(), func=AF.Sigmoid),
                            reads=[RB[bank]], writes=[Rsr], sem="act", inc=1)
            p.op("dve", lambda h: h.tensor_tensor_scan(
                st_t.ap(), dec_t.ap().broadcast_to([128, T]), kv_t.ap(),
                0.0, ALU.mult, ALU.add),
                reads=[Rkv, Rdec], writes=[Rst], sem="dve", inc=1)
            p.op("dve", lambda h, e=e: h.tensor_mul(yv(e), sr_t.ap(), st_t.ap()),
                 reads=[Rsr, Rst], writes=[RY[e]], sem="dve", inc=1)
        if post_do is not None:
            p.fence([Rsr, Rks], [Rxstr, Rxstrb])
        for do in range(NCD):
            tile, wo_regs = next_piece(1024)
            for tt in range(NT):
                bank = 6 + (tt % 2)
                pieces = []
                for e in range(NCD):
                    lt = tile(e * 128, (e + 1) * 128)
                    pieces.append((lt, yv(e, tt), wo_regs + [RY[e]]))
                group(bank, TTOK, pieces)
                p.op("dve", lambda h, do=do, tt=tt, bank=bank: h.tensor_add(
                    H(do, tt), H(do, tt), pb[bank].ap()),
                    reads=[RH[do][tt], RB[bank]], writes=[RH[do][tt]],
                    sem="dve", inc=1)
            if post_do is not None:
                post_do(do)

    def emit_router(seq):
        tile, rt_regs = next_piece(128)
        for tc in range(NTC):
            bank = 0 + (tc % 2)
            pieces = []
            for c in range(NCD):
                pieces.append((nm128(c, tc), tile(c * NE, (c + 1) * NE),
                               rt_regs + [RN[c][tc * 128 // TTOK]]))
            group(bank, NE, pieces)
            p.op("dve", lambda h, tc=tc, bank=bank: h.tensor_add(
                lg_all.ap()[:, tc * NE:(tc + 1) * NE], pb[bank].ap()[:, 0:NE],
                biasbc.ap()),
                reads=[RB[bank], Rbbc], writes=[Rlog], sem="dve", inc=1)
        L3 = lg_all.ap().rearrange("p (c e) -> p c e", e=NE)
        M13 = m16a.ap().unsqueeze(2).broadcast_to([128, NTC, NE])
        M23 = m16b.ap().unsqueeze(2).broadcast_to([128, NTC, NE])
        R13 = rw1_16.ap().unsqueeze(2).broadcast_to([128, NTC, NE])
        R23 = rw2_16.ap().unsqueeze(2).broadcast_to([128, NTC, NE])
        D3 = maskda.ap().rearrange("p (c e) -> p c e", e=NE)
        p.op("dve", lambda h: h.reduce_max(m16a.ap(), L3, axis=AXX),
             reads=[Rlog], writes=[Rtop], sem="dve", inc=1)
        p.op("dve", lambda h: h.tensor_tensor(mask1a.ap().rearrange(
            "p (c e) -> p c e", e=NE), L3, M13, ALU.is_ge),
            reads=[Rlog, Rtop], writes=[Rtop], sem="dve", inc=1)
        p.op("dve", lambda h: h.scalar_tensor_tensor(
            maskda.ap(), mask1a.ap(), -3.0e38, lg_all.ap(), ALU.mult, ALU.add),
            reads=[Rlog, Rtop], writes=[Rtop], sem="dve", inc=1)
        p.op("dve", lambda h: h.reduce_max(m16b.ap(), D3, axis=AXX),
             reads=[Rtop], writes=[Rtop], sem="dve", inc=1)
        p.op("dve", lambda h: h.tensor_tensor(tmpa.ap().rearrange(
            "p (c e) -> p c e", e=NE), D3, M23, ALU.is_ge),
            reads=[Rtop], writes=[Rtop], sem="dve", inc=1)
        p.op("dve", lambda h: h.tensor_sub(m16a.ap(), m16a.ap(), m16b.ap()),
             reads=[Rtop], writes=[Rtop], sem="dve", inc=1)
        p.op("act", lambda h: h.activation(out=rw1_16.ap(), in_=m16a.ap(),
                                           func=AF.Sigmoid),
             reads=[Rtop], writes=[Rtop], sem="act", inc=1)
        p.op("dve", lambda h: h.tensor_scalar(rw2_16.ap(), rw1_16.ap(), -1.0,
                                              1.0, ALU.mult, ALU.add),
             reads=[Rtop], writes=[Rtop], sem="dve", inc=1)
        p.op("dve", lambda h: h.tensor_tensor(mask1a.ap().rearrange(
            "p (c e) -> p c e", e=NE), mask1a.ap().rearrange(
            "p (c e) -> p c e", e=NE), R13, ALU.mult),
            reads=[Rtop], writes=[Rtop], sem="dve", inc=1)
        p.op("dve", lambda h: h.tensor_tensor(tmpa.ap().rearrange(
            "p (c e) -> p c e", e=NE), tmpa.ap().rearrange(
            "p (c e) -> p c e", e=NE), R23, ALU.mult),
            reads=[Rtop], writes=[Rtop], sem="dve", inc=1)
        p.op("dve", lambda h: h.tensor_add(cw_all.ap(), mask1a.ap(), tmpa.ap()),
             reads=[Rtop], writes=[Rcwall], sem="dve", inc=1)
        p.op("pe", lambda h: h.transpose(pb[2].ap()[:, 0:128], cw_all.ap(),
                                         ident.ap()),
             reads=[Rcwall, Rident], writes=[RB[2]], sem="pe", inc=1)
        p.op("act", lambda h: h.activation(out=cwT.ap(), in_=pb[2].ap()[:, 0:128],
                                           func=AF.Copy),
             reads=[RB[2]], writes=[RcwT], sem="act", inc=1)
        p.op("sp", lambda h: h.dma_start(out=scrD, in_=cwT.ap()),
             reads=[RcwT], writes=[RscrD], sem=p.dma_sem("sq"), inc=16)

    def emit_cw_prep(ex):
        src = scrD.rearrange("(c e) q -> e c q", e=NE)[ex].unsqueeze(0)
        dst = cw_row.ap().rearrange("p (c q) -> p c q", c=NTC)
        p.op("pool", lambda h, src=src, dst=dst: h.dma_start(out=dst, in_=src),
             reads=[RscrD], writes=[Rcwrow], sem=p.dma_sem("wq"), inc=16)
        for tt in range(NT):
            bank = 6 + (tt % 2)
            p.op("pe", lambda h, tt=tt, bank=bank: h.matmul(
                pb[bank].ap(), onesrowb.ap(),
                cw_row.ap()[:, tt * TTOK:(tt + 1) * TTOK], start=True, stop=True),
                reads=[Rcwrow, Ronesb], writes=[RB[bank]], sem="pe", inc=1)
            p.op("act", lambda h, tt=tt, bank=bank: h.activation(
                out=cw_bc.ap()[:, tt * TTOK:(tt + 1) * TTOK],
                in_=pb[bank].ap(), func=AF.Copy),
                reads=[RB[bank]], writes=[Rcwbc], sem="act", inc=1)

    def emit_expert(ex):
        if ex >= 0:
            emit_cw_prep(ex)
        for h1 in range(NCH):
            tile, w_regs = next_piece(2048)
            for tt in range(NT):
                bank = 0 + (tt % 2)
                pieces = []
                for dch in range(NCD):
                    lt = tile(dch * 128, (dch + 1) * 128)
                    pieces.append((lt, nm(dch, tt), w_regs + [RN[dch][tt]]))
                group(bank, TTOK, pieces)
                p.op("act", lambda h, tt=tt, bank=bank: h.activation(
                    out=silu_t[tt % 2].ap(), in_=pb[bank].ap(), func=AF.Silu),
                    reads=[RB[bank]], writes=[Rsilu[tt % 2]], sem="act", inc=1)
                banku = 2 + (tt % 2)
                pieces = []
                for dch in range(NCD):
                    lt = tile(1024 + dch * 128, 1024 + (dch + 1) * 128)
                    pieces.append((lt, nm(dch, tt), w_regs + [RN[dch][tt]]))
                group(banku, TTOK, pieces)
                p.op("dve", lambda h, h1=h1, tt=tt, banku=banku: h.tensor_mul(
                    eo(h1, tt), silu_t[tt % 2].ap(), pb[banku].ap()),
                    reads=[Rsilu[tt % 2], RB[banku]], writes=[REO[h1]],
                    sem="dve", inc=1)
        for do in range(NCD):
            tile, w_regs = next_piece(2048)
            for tt in range(NT):
                bank = 4 + (tt % 2)
                pieces = []
                for h1 in range(NCH):
                    lt = tile(h1 * 128, (h1 + 1) * 128)
                    pieces.append((lt, eo(h1, tt), w_regs + [REO[h1]]))
                group(bank, TTOK, pieces)
                if ex >= 0:
                    p.op("dve", lambda h, tt=tt, bank=bank: h.tensor_mul(
                        tmpm[tt % 2].ap(), pb[bank].ap(),
                        cw_bc.ap()[:, tt * TTOK:(tt + 1) * TTOK]),
                        reads=[RB[bank], Rcwbc], writes=[Rtmpm[tt % 2]],
                        sem="dve", inc=1)
                    p.op("dve", lambda h, do=do, tt=tt: h.tensor_add(
                        H(do, tt), H(do, tt), tmpm[tt % 2].ap()),
                        reads=[RH[do][tt], Rtmpm[tt % 2]], writes=[RH[do][tt]],
                        sem="dve", inc=1)
                else:
                    p.op("dve", lambda h, do=do, tt=tt, bank=bank: h.tensor_add(
                        H(do, tt), H(do, tt), pb[bank].ap()),
                        reads=[RH[do][tt], RB[bank]], writes=[RH[do][tt]],
                        sem="dve", inc=1)

    def emit_seq(seq):
        rwkv_regs = [RY[e] for e in range(NCD)] + [Rkv, Rst, Rsr, Rks, Rxstrb]
        row_regs = list(Rrows)
        ffn_regs = list(REO) + list(Rsilu) + list(Rtmpm) + [Rcwrow, Rcwbc]
        ab_regs = Rabsb + Rbbsb + Rbfsb + Rafsb
        p.fence(ffn_regs, rwkv_regs + row_regs + ab_regs + [Rxstr])

        l0_pieces = [next_piece(3 * 1024) for _ in range(2)]
        for c in range(NCD):
            srcx = xin[seq].rearrange("(c q) t -> c q t", c=NCD)[c]
            p.op("sp", lambda h, c=c, srcx=srcx: h.dma_start(out=H(c), in_=srcx),
                 reads=[], writes=list(RH[c]), sem=p.dma_sem("sq"), inc=16)
        emit_ln("ln1", 0, seq, lambda c, tt: H(c, tt), lambda c, tt: [RH[c][tt]])
        if seq == 0:
            tap("h0", Hb.ap(), sum(RH, []))

        def final_add(do):
            buf = (xstr, Rxstr) if do % 2 == 0 else (xstr_b, Rxstrb)
            srcx = xin[seq].rearrange("(c q) t -> c q t", c=NCD)[do]
            p.op("sp", lambda h, srcx=srcx, buf=buf: h.dma_start(
                out=buf[0].ap(), in_=srcx),
                reads=[], writes=[buf[1]], sem=p.dma_sem("sq"), inc=16)
            p.op("dve", lambda h, do=do, buf=buf: h.tensor_add(
                H(do), H(do), buf[0].ap()),
                reads=RH[do] + [buf[1]], writes=list(RH[do]), sem="dve", inc=1)

        for l in range(NLAY):
            emit_rwkv_layer(l, seq, kvr_pieces=l0_pieces if l == 0 else None,
                            post_do=final_add if l == NLAY - 1 else None)
            if seq == 0:
                tap("hl%d" % l, Hb.ap(), sum(RH, []))
                if l == 0:
                    tap("nm0", NM.ap(), sum(RN, []))
                    tap("y0", y_t.ap(), list(RY))
                    tap("ks7", ks_t.ap(), [Rks])
                    tap("kv7", kv_t.ap(), [Rkv])
                    tap("st7", st_t.ap(), [Rst])
                    tap("sr7", sr_t.ap(), [Rsr])
        if seq == 0:
            tap("x1", Hb.ap(), sum(RH, []))
        p.fence([Rkv, Rst], Rabsb + Rbbsb + Rbfsb + Rafsb)
        emit_ln("out", 0, seq, lambda c, tt: H(c, tt), lambda c, tt: [RH[c][tt]])
        if seq == 0:
            tap("nm2", NM.ap(), sum(RN, []))
        emit_router(seq)
        if seq == 0:
            tap("cwa", cw_all.ap(), [Rcwall])
        p.fence(rwkv_regs + row_regs + ab_regs + [Rxstr], ffn_regs)
        for ex in range(-1, NE):
            emit_expert(ex)
            if seq == 0 and ex == -1:
                tap("hsh", Hb.ap(), sum(RH, []))
        for c in range(NCD):
            dst = xout[seq].rearrange("(c q) t -> c q t", c=NCD)[c]
            p.op("sp", lambda h, c=c, dst=dst: h.dma_start(out=dst, in_=H(c)),
                 reads=list(RH[c]), writes=[], sem=p.dma_sem("sq"), inc=16)

    init()
    for seq in range(BL):
        soff_ctr[0] = 0
        emit_seq(seq)
    p.wait("sp", {f"sq{i}": p.cnt[f"sq{i}"] for i in range(NSQ)}, force=True)

    def runner(qname):
        def f(h):
            for emit, waits, mark in p.q[qname]:
                for s, c in waits:
                    h.wait_ge(sems[s], c)
                if emit is not None:
                    inst = emit(h)
                    if mark is not None:
                        inst.then_inc(sems[mark[0]], mark[2])
        return f

    with nc.Block() as blk:
        blk.sync(runner("sp"))
        blk.gpsimd(runner("pool"))
        blk.tensor(runner("pe"))
        blk.vector(runner("dve"))
        blk.scalar(runner("act"))

    es.close()
    return nc


_NC_CACHE = {}


def kernel(**inputs):
    x = np.ascontiguousarray(np.asarray(inputs["L0_attn_h"], np.float32))
    ws = pack_stream(inputs)
    lnw = np.asarray(inputs["L0_attn_ln_w"], np.float32)
    lnb = np.asarray(inputs["L0_attn_ln_b"], np.float32)
    bias = np.asarray(inputs["L0_ffn_experts_bias"], np.float32)
    lnw_cols = np.ascontiguousarray(lnw.reshape(NLAY, NCD, 128).transpose(0, 2, 1))
    lnb_cols = np.ascontiguousarray(lnb.reshape(NLAY, NCD, 128).transpose(0, 2, 1))
    ident = np.eye(128, dtype=np.float32)
    bias_row = np.ascontiguousarray(bias.reshape(1, NE))

    xbdt = np.ascontiguousarray(x.transpose(0, 2, 1))

    key = "nc_dbg" if DEBUG_TAPS else "nc"
    if key not in _NC_CACHE:
        _NC_CACHE[key] = build_nc(debug=DEBUG_TAPS)
    nc = _NC_CACHE[key]

    in_maps = []
    for c in range(NCORE):
        in_maps.append({
            "xin": np.ascontiguousarray(xbdt[c * BL:(c + 1) * BL]),
            "wstream": ws,
            "ident": ident,
            "lnwcols": lnw_cols,
            "lnbcols": lnb_cols,
            "biasrow": bias_row,
        })
    res = run_bass_kernel_spmd(nc, in_maps, core_ids=list(range(NCORE)))
    out = np.empty((B, T, D), np.float32)
    for c in range(NCORE):
        o = np.asarray(res.results[c]["xout"], np.float32)
        out[c * BL:(c + 1) * BL] = o.transpose(0, 2, 1)
    if DEBUG_TAPS:
        kernel._last_taps = [res.results[c] for c in range(NCORE)]
    return out


# version 23
# speedup vs baseline: 1.1214x; 1.0683x over previous
import numpy as np
import ml_dtypes
import concourse.bass as bass
import concourse.mybir as mybir
from concourse.bass_utils import run_bass_kernel_spmd
from contextlib import ExitStack

F32 = mybir.dt.float32
BF16 = mybir.dt.bfloat16
U8 = mybir.dt.uint8
AF = mybir.ActivationFunctionType
ALU = mybir.AluOpType
AXX = mybir.AxisListType.X
BFNP = ml_dtypes.bfloat16

B, T, D, HD, NE, NLAY = 16, 2048, 1024, 2048, 8, 2
NCORE = 8
BL = B // NCORE
NCD = D // 128
NCH = HD // 128
TTOK = 512
NT = T // TTOK
NTC = T // 128
DEC = 0.95
EPS = 1e-5
INV_D = 1.0 / D
RING = 8448
RSLOT = 64

ENGS = ("pe", "dve", "act", "pool", "sp")
NWQ, NSQ = 16, 8
SEMS = ("pe", "dve", "act", "pl") + tuple(f"wq{i}" for i in range(NWQ)) + \
    tuple(f"sq{i}" for i in range(NSQ))


def make_plan():
    p = []
    for l in range(NLAY):
        for e in range(NCD):
            p.append(("kvr", l, e, 3 * 1024))
        for do in range(NCD):
            p.append(("wo", l, do, 1024))
    p.append(("router", 0, 0, 128))
    for ex in range(-1, NE):
        for h1 in range(NCH):
            p.append(("w13", ex, h1, 2048))
        for do in range(NCD):
            p.append(("w2", ex, do, 2048))
    return p


def _tile128(mat):
    K, C = mat.shape
    n = K // 128
    return mat.reshape(n, 128, C).transpose(1, 0, 2).reshape(128, n * C)


def pack_stream(ins):
    Wk, Wv, Wr, Wo = (np.asarray(ins[k], np.float32) for k in
                      ("L0_attn_Wk", "L0_attn_Wv", "L0_attn_Wr", "L0_attn_Wo"))
    rw = np.asarray(ins["L0_ffn_router_w"], np.float32)
    w1 = np.asarray(ins["L0_ffn_w1"], np.float32)
    w3 = np.asarray(ins["L0_ffn_w3"], np.float32)
    w2 = np.asarray(ins["L0_ffn_w2"], np.float32)
    sw1 = np.asarray(ins["L0_ffn_sw1"], np.float32)
    sw3 = np.asarray(ins["L0_ffn_sw3"], np.float32)
    sw2 = np.asarray(ins["L0_ffn_sw2"], np.float32)
    WkT = [Wk[l].T.copy() for l in range(NLAY)]
    WvT = [Wv[l].T.copy() for l in range(NLAY)]
    WrT = [Wr[l].T.copy() for l in range(NLAY)]
    WoT = [Wo[l].T.copy() for l in range(NLAY)]
    RT = rw.T.copy()
    cols = []
    for kind, a, b, n in make_plan():
        if kind == "kvr":
            l, e = a, b
            sl = slice(e * 128, (e + 1) * 128)
            cols.append(np.concatenate(
                [_tile128(WkT[l][:, sl]), _tile128(WvT[l][:, sl]),
                 _tile128(WrT[l][:, sl])], axis=1))
        elif kind == "wo":
            l, do = a, b
            cols.append(_tile128(WoT[l][:, do * 128:(do + 1) * 128]))
        elif kind == "router":
            rt = np.zeros((128, 128), np.float32)
            rt[:, :64] = _tile128(RT)
            cols.append(rt)
        elif kind == "w13":
            ex, h1 = a, b
            m1 = sw1[0] if ex < 0 else w1[ex]
            m3 = sw3[0] if ex < 0 else w3[ex]
            sl = slice(h1 * 128, (h1 + 1) * 128)
            cols.append(np.concatenate(
                [_tile128(m1[:, sl]), _tile128(m3[:, sl])], axis=1))
        elif kind == "w2":
            ex, do = a, b
            m2 = sw2[0] if ex < 0 else w2[ex]
            cols.append(_tile128(m2[:, do * 128:(do + 1) * 128]))
    ws = np.concatenate(cols, axis=1)
    return np.ascontiguousarray(ws)


class Reg:
    __slots__ = ("w", "r")

    def __init__(self):
        self.w = None
        self.r = {}


class Prog:
    def __init__(self):
        self.q = {e: [] for e in ENGS}
        self.cnt = {s: 0 for s in SEMS}
        self.waited = {e: {} for e in ENGS}
        self._rot = {"wq": 0, "sq": 0}

    def dma_sem(self, queue):
        n = {"wq": NWQ, "sq": NSQ}[queue]
        i = self._rot[queue]
        self._rot[queue] = (i + 1) % n
        s = f"{queue}{i}"
        if self.cnt[s] > 0:
            eng = {"wq": "pool", "sq": "sp"}[queue]
            self.wait(eng, {s: self.cnt[s]}, force=True)
        return s

    def op(self, eng, emit, reads=(), writes=(), sem=None, inc=0, rmark=None,
           war=()):
        deps = {}
        for r in reads:
            if r.w:
                deps[r.w[0]] = max(deps.get(r.w[0], 0), r.w[1])
        for w in writes:
            if w.w:
                deps[w.w[0]] = max(deps.get(w.w[0], 0), w.w[1])
            for s, c in w.r.items():
                deps[s] = max(deps.get(s, 0), c)
        for w in war:
            if w.w:
                deps[w.w[0]] = max(deps.get(w.w[0], 0), w.w[1])
            for s, c in w.r.items():
                deps[s] = max(deps.get(s, 0), c)
        own = {"pe": "pe", "sp": "sq", "pool": "wq"}.get(eng)
        if own is not None:
            for s in [s for s in deps if s.startswith(own) or s == own]:
                deps.pop(s)
        wt = self.waited[eng]
        waits = []
        for s in sorted(deps):
            c = deps[s]
            if c > 0 and wt.get(s, 0) < c:
                waits.append((s, c))
                wt[s] = c
        if sem is not None and inc:
            self.cnt[sem] += inc
            mark = (sem, self.cnt[sem], inc)
        else:
            mark = None
        self.q[eng].append((emit, waits, mark))
        rm = (mark[0], mark[1]) if mark else rmark
        if rm:
            for r in reads:
                r.r[rm[0]] = max(r.r.get(rm[0], 0), rm[1])
            for w in writes:
                w.w = rm
                w.r = {}
        return mark

    def wait(self, eng, deps, force=False):
        own = {"pe": "pe", "sp": "sq", "pool": "wq"}.get(eng)
        if own is not None and not force:
            for s in [s for s in deps if s.startswith(own) or s == own]:
                deps.pop(s)
        wt = self.waited[eng]
        waits = []
        for s in sorted(deps):
            c = deps[s]
            if c > 0 and wt.get(s, 0) < c:
                waits.append((s, c))
                wt[s] = c
        if waits:
            self.q[eng].append((None, waits, None))

    def fence(self, src_regs, dst_regs):
        deps = {}
        for r in src_regs:
            if r.w:
                deps[r.w[0]] = max(deps.get(r.w[0], 0), r.w[1])
            for s, c in r.r.items():
                deps[s] = max(deps.get(s, 0), c)
        for d in dst_regs:
            for s, c in deps.items():
                d.r[s] = max(d.r.get(s, 0), c)


DEBUG_TAPS = False

def build_nc(debug=False):
    nc = bass.Bass("TRN2", target_bir_lowering=False)
    plan = make_plan()
    WTOT = sum(it[3] for it in plan)
    taps = {}
    if debug:
        for nmx, shp, dt in (("ks7", [128, T], BF16), ("kv7", [128, T], F32),
                             ("st7", [128, T], BF16), ("sr7", [128, T], BF16),
                             ("h0", [128, NCD * T], F32), ("hl0", [128, NCD * T], F32),
                             ("hl1", [128, NCD * T], F32), ("x1", [128, NCD * T], F32),
                             ("nm0", [128, NCD * T], BF16), ("nm2", [128, NCD * T], BF16),
                             ("cwa", [128, NTC * NE], F32), ("hsh", [128, NCD * T], F32),
                             ("y0", [128, NCD * T], BF16)):
            taps[nmx] = nc.dram_tensor("tap_" + nmx, shp, dt, kind="ExternalOutput").ap()

    xin = nc.dram_tensor("xin", [BL, D, T], F32, kind="ExternalInput").ap()
    ws = nc.dram_tensor("wstream", [128, WTOT], F32, kind="ExternalInput").ap()
    ident_in = nc.dram_tensor("ident", [128, 128], F32, kind="ExternalInput").ap()
    lnw_in = nc.dram_tensor("lnwcols", [NLAY, 128, NCD], F32, kind="ExternalInput").ap()
    lnb_in = nc.dram_tensor("lnbcols", [NLAY, 128, NCD], F32, kind="ExternalInput").ap()
    bias_in = nc.dram_tensor("biasrow", [1, NE], F32, kind="ExternalInput").ap()
    xout = nc.dram_tensor("xout", [BL, D, T], F32, kind="ExternalOutput").ap()
    scrD = nc.dram_tensor("scrD", [128, 128], F32, kind="Internal").ap()

    es = ExitStack()
    Hb = nc.alloc_sbuf_tensor("Hb", [128, NCD * T], F32)
    NM = nc.alloc_sbuf_tensor("NM", [128, NCD * T], BF16)
    ring = nc.alloc_sbuf_tensor("ring", [128, RING], BF16)
    ident = nc.alloc_sbuf_tensor("ident_s", [128, 128], F32)
    cw_all = nc.alloc_sbuf_tensor("cw_all", [128, NTC * NE], F32)
    cwT = nc.alloc_sbuf_tensor("cwT", [128, 128], F32)
    onesrow = nc.alloc_sbuf_tensor("onesrow", [1, 128], F32)
    onesrowb = nc.alloc_sbuf_tensor("onesrowb", [1, 128], BF16)
    onescolb = nc.alloc_sbuf_tensor("onescolb", [128, 1], BF16)
    dec_t = nc.alloc_sbuf_tensor("dec_t", [128, 1], F32)
    epsc = nc.alloc_sbuf_tensor("epsc", [128, 1], F32)
    nc.const_aps.register(F32, EPS, epsc.ap())
    wcols = nc.alloc_sbuf_tensor("wcols", [128, NLAY * NCD], F32)
    bcols = nc.alloc_sbuf_tensor("bcols", [128, NLAY * NCD], F32)
    biasrow = nc.alloc_sbuf_tensor("biasrow_s", [1, NE], F32)
    biasbc = nc.alloc_sbuf_tensor("biasbc", [128, NE], F32)
    lg_all = nc.alloc_sbuf_tensor("lg_all", [128, NTC * NE], F32)
    m16a = nc.alloc_sbuf_tensor("m16a", [128, NTC], F32)
    m16b = nc.alloc_sbuf_tensor("m16b", [128, NTC], F32)
    rw1_16 = nc.alloc_sbuf_tensor("rw1_16", [128, NTC], F32)
    rw2_16 = nc.alloc_sbuf_tensor("rw2_16", [128, NTC], F32)
    mask1a = nc.alloc_sbuf_tensor("mask1a", [128, NTC * NE], F32)
    maskda = nc.alloc_sbuf_tensor("maskda", [128, NTC * NE], F32)
    tmpa = nc.alloc_sbuf_tensor("tmpa", [128, NTC * NE], F32)
    xb = nc.alloc_sbuf_tensor("xb", [128, 2 * TTOK], BF16)
    x2b = nc.alloc_sbuf_tensor("x2b", [128, 2 * TTOK], BF16)
    t1f = nc.alloc_sbuf_tensor("t1f", [128, TTOK], F32)
    t2f = nc.alloc_sbuf_tensor("t2f", [128, TTOK], F32)

    arena = nc.alloc_sbuf_tensor("arena", [128, 81920], U8)
    base = nc.lookup_mloc(arena).addr
    at = nc.alloc_sbuf_tensor_at
    eo_t = at("eo", [128, NCH * T], BF16, offset=base + 0)
    y_t = at("yt", [128, NCD * T], BF16, offset=base + 0)
    kv_t = at("kvt", [128, T], F32, offset=base + 32768)
    absb = [at(f"absb{i}", [128, TTOK], BF16, offset=base + 32768 + i * 1024)
            for i in range(2)]
    bbsb = [at(f"bbsb{i}", [128, TTOK], BF16, offset=base + 34816 + i * 1024)
            for i in range(2)]
    bfsb = [at(f"bfsb{i}", [128, TTOK], F32, offset=base + 36864 + i * 2048)
            for i in range(2)]
    afsb = [at(f"afsb{i}", [128, TTOK], F32, offset=base + 40960 + i * 2048)
            for i in range(2)]
    st_t = at("stt", [128, T], BF16, offset=base + 40960)
    sr_t = at("srt", [128, T], BF16, offset=base + 45056)
    ks_t = at("kst", [128, T], BF16, offset=base + 49152)
    r_sum = at("r_sum", [1, TTOK], F32, offset=base + 53248)
    r_sq = at("r_sq", [1, TTOK], F32, offset=base + 55296)
    r_t = at("r_t", [1, TTOK], F32, offset=base + 57344)
    r_rstd2 = [at(f"r_rstd{i}", [1, TTOK], F32, offset=base + 59392 + i * 2048)
               for i in range(2)]
    r_nmr2 = [at(f"r_nmr{i}", [1, TTOK], F32, offset=base + 63488 + i * 2048)
              for i in range(2)]
    xstr = at("xstr", [128, T], F32, offset=base + 67584)
    xstr_b = at("xstr_b", [128, T], F32, offset=base + 45056)
    silu_t = [at(f"silu{i}", [128, TTOK], BF16, offset=base + 67584 + i * 1024)
              for i in range(2)]
    tmpm = [at(f"tmpm{i}", [128, TTOK], F32, offset=base + 69632 + i * 2048)
            for i in range(2)]
    cw_row = at("cw_row", [1, T], BF16, offset=base + 73728)
    cw_bc = at("cw_bc", [128, T], BF16, offset=base + 77824)

    pb = [nc.alloc_psum_tensor(f"pb{i}", [128, 512], F32) for i in range(8)]

    sems = {}
    for s in SEMS:
        sems[s] = es.enter_context(nc.semaphore(s))

    p = Prog()

    RH = [[Reg() for _ in range(NT)] for _ in range(NCD)]
    RN = [[Reg() for _ in range(NT)] for _ in range(NCD)]
    RY = [Reg() for _ in range(NCD)]
    REO = [Reg() for _ in range(NCH)]
    RB = [Reg() for _ in range(8)]
    Rkv, Rst, Rsr, Rks = Reg(), Reg(), Reg(), Reg()
    Rrows = [Reg() for _ in range(3)]
    Rrstd2 = [Reg(), Reg()]
    Rnmr2 = [Reg(), Reg()]
    Rxstr, Rxstrb, Rt1, Rt2 = Reg(), Reg(), Reg(), Reg()
    Rxb = [Reg(), Reg()]
    Rx2b = [Reg(), Reg()]
    Rabsb = [Reg(), Reg()]
    Rbbsb = [Reg(), Reg()]
    Rbfsb = [Reg(), Reg()]
    Rafsb = [Reg(), Reg()]
    Rsilu = [Reg(), Reg()]
    Rtmpm = [Reg(), Reg()]
    Rcwrow, Rcwbc, Rcwall, RcwT, RscrD = Reg(), Reg(), Reg(), Reg(), Reg()
    Rdec, Reps, Rones, Ronesb, Ronescol = Reg(), Reg(), Reg(), Reg(), Reg()
    Rident, Rbrow, Rbbc = Reg(), Reg(), Reg()
    Rwcols = [Reg() for _ in range(NLAY)]
    Rbcols = [Reg() for _ in range(NLAY)]
    Rtop = Reg()
    Rlog = Reg()

    nslot = RING // RSLOT
    ring_slot_free = [Reg() for _ in range(nslot)]

    def H(c, tt=None):
        a = Hb.ap()
        return a[:, c * T:(c + 1) * T] if tt is None else \
            a[:, c * T + tt * TTOK: c * T + (tt + 1) * TTOK]

    def nm(c, tt=None):
        a = NM.ap()
        if tt is None:
            return a[:, c * T:(c + 1) * T]
        return a[:, c * T + tt * TTOK: c * T + (tt + 1) * TTOK]

    def nm128(c, tc):
        return NM.ap()[:, c * T + tc * 128: c * T + tc * 128 + 128]

    def eo(h1, tt):
        return eo_t.ap()[:, h1 * T + tt * TTOK: h1 * T + (tt + 1) * TTOK]

    def yv(e, tt=None):
        a = y_t.ap()
        if tt is None:
            return a[:, e * T:(e + 1) * T]
        return a[:, e * T + tt * TTOK: e * T + (tt + 1) * TTOK]

    ring_pos = [0]

    def ring_alloc(ncols, soff):
        segs = []
        left = ncols
        cur = soff
        while left > 0:
            pos = ring_pos[0]
            n = min(left, RING - pos)
            segs.append((pos, n, cur))
            ring_pos[0] = (pos + n) % RING
            left -= n
            cur += n
        return segs

    def emit_piece_dma(ncols, soff):
        segs = ring_alloc(ncols, soff)
        regs = []
        for (roff, n, so) in segs:
            sreg = Reg()
            s0 = roff // RSLOT
            s1 = (roff + n - 1) // RSLOT
            prev, seen = [], set()
            for i in range(s0, s1 + 1):
                r0 = ring_slot_free[i]
                if r0 is not None and id(r0) not in seen:
                    seen.add(id(r0))
                    prev.append(r0)
                    assert r0.r or r0.w is None, (
                        "ring WAR hazard: overwriting unconsumed piece")

            def emit(h, roff=roff, n=n, so=so):
                return h.dma_start(out=ring.ap()[:, roff:roff + n],
                                   in_=ws[:, so:so + n])
            p.op("pool", emit, reads=[], writes=[sreg], war=prev,
                 sem=p.dma_sem("wq"), inc=16)
            for i in range(s0, s1 + 1):
                ring_slot_free[i] = sreg
            regs.append(sreg)
        base_off = segs[0][0]
        contig = len(segs) == 1

        def tile(a, b):
            if contig:
                return ring.ap()[:, base_off + a: base_off + b]
            acc = 0
            for (roff, n, _so) in segs:
                if a >= acc and b <= acc + n:
                    return ring.ap()[:, roff + (a - acc): roff + (b - acc)]
                acc += n
            raise AssertionError("ring tile spans wrap; adjust RSLOT/RING")
        return tile, regs

    def group(bank, outw, pieces):
        n = len(pieces)
        endmark = ("pe", p.cnt["pe"] + 1)
        for i, (lt, rh, regs) in enumerate(pieces):
            first, last = i == 0, i == n - 1
            m = lt.shape[-1]

            def emit(h, lt=lt, rh=rh, first=first, last=last, bank=bank, m=m, outw=outw):
                return h.matmul(pb[bank].ap()[0:m, 0:outw], lt, rh,
                                start=first, stop=last)
            if last:
                p.op("pe", emit, reads=list(regs), writes=[RB[bank]], sem="pe", inc=1)
            else:
                p.op("pe", emit, reads=list(regs),
                     writes=[RB[bank]] if first else [], rmark=endmark)
        return endmark

    def tap(name, src_t, regs):
        if not debug or name not in taps:
            return
        p.op("sp", lambda h: h.dma_start(out=taps[name], in_=src_t),
             reads=regs, writes=[], sem=p.dma_sem("sq"), inc=16)

    def init():
        p.op("dve", lambda h: h.memset(dec_t.ap(), DEC), writes=[Rdec], sem="dve", inc=1)
        p.op("dve", lambda h: h.memset(epsc.ap(), EPS), writes=[Reps], sem="dve", inc=1)
        p.op("dve", lambda h: h.memset(onesrow.ap(), 1.0), writes=[Rones], sem="dve", inc=1)
        p.op("dve", lambda h: h.memset(onesrowb.ap(), 1.0), writes=[Ronesb], sem="dve", inc=1)
        p.op("dve", lambda h: h.memset(onescolb.ap(), 1.0), writes=[Ronescol], sem="dve", inc=1)
        p.op("sp", lambda h: h.dma_start(out=ident.ap(), in_=ident_in),
             writes=[Rident], sem=p.dma_sem("sq"), inc=16)
        for l in range(NLAY):
            p.op("sp", lambda h, l=l: h.dma_start(
                out=wcols.ap()[:, l * NCD:(l + 1) * NCD], in_=lnw_in[l]),
                writes=[Rwcols[l]], sem=p.dma_sem("sq"), inc=16)
            p.op("sp", lambda h, l=l: h.dma_start(
                out=bcols.ap()[:, l * NCD:(l + 1) * NCD], in_=lnb_in[l]),
                writes=[Rbcols[l]], sem=p.dma_sem("sq"), inc=16)
        p.op("sp", lambda h: h.dma_start(out=biasrow.ap(), in_=bias_in),
             writes=[Rbrow], sem=p.dma_sem("sq"), inc=16)
        p.op("pe", lambda h: h.matmul(pb[0].ap()[:, 0:NE], onesrow.ap(),
                                      biasrow.ap(), start=True, stop=True),
             reads=[Rones, Rbrow], writes=[RB[0]], sem="pe", inc=1)
        p.op("act", lambda h: h.activation(out=biasbc.ap(), in_=pb[0].ap()[:, 0:NE],
                                           func=AF.Copy),
             reads=[RB[0]], writes=[Rbbc], sem="act", inc=1)

    def emit_ln(kind, l, seq, src, src_regs_fn):
        nmout = kind != "ln1"

        def stats(tt):
            bs, bq = 0 + (tt % 2), 2 + (tt % 2)
            for c in range(NCD):
                sl = (c % 2)
                x2a = x2b.ap()[:, sl * TTOK:(sl + 1) * TTOK]
                sregs = src_regs_fn(c, tt)
                if nmout:
                    cast_dst, cast_reg = nm(c, tt), RN[c][tt]
                else:
                    cast_dst = xb.ap()[:, sl * TTOK:(sl + 1) * TTOK]
                    cast_reg = Rxb[sl]
                p.op("act", lambda h, c=c, tt=tt, cast_dst=cast_dst:
                     h.activation(out=cast_dst, in_=src(c, tt), func=AF.Copy),
                     reads=sregs, writes=[cast_reg], sem="act", inc=1)
                p.op("dve", lambda h, x2a=x2a, cast_dst=cast_dst: h.tensor_mul(
                    x2a, cast_dst, cast_dst),
                    reads=[cast_reg], writes=[Rx2b[sl]], sem="dve", inc=1)
                first, last = c == 0, c == NCD - 1
                p.op("pe", lambda h, first=first, last=last, bs=bs,
                     cast_dst=cast_dst: h.matmul(
                         pb[bs].ap()[0:1, :], onescolb.ap(), cast_dst,
                         start=first, stop=last),
                     reads=[cast_reg, Ronescol],
                     writes=[RB[bs]] if (first or last) else [],
                     sem="pe", inc=1)
                p.op("pe", lambda h, first=first, last=last, bq=bq, x2a=x2a:
                     h.matmul(pb[bq].ap()[0:1, :], onescolb.ap(), x2a,
                              start=first, stop=last),
                     reads=[Rx2b[sl], Ronescol],
                     writes=[RB[bq]] if (first or last) else [],
                     sem="pe", inc=1)

        def rows(tt):
            bs, bq, pr = 0 + (tt % 2), 2 + (tt % 2), tt % 2
            rstd_a, nmr_a = r_rstd2[pr].ap(), r_nmr2[pr].ap()
            p.op("act", lambda h, bs=bs: h.activation(
                out=r_sum.ap(), in_=pb[bs].ap()[0:1, :], func=AF.Copy,
                scale=INV_D),
                reads=[RB[bs]], writes=[Rrows[0]], sem="act", inc=1)
            p.op("act", lambda h, bq=bq: h.activation(
                out=r_sq.ap(), in_=pb[bq].ap()[0:1, :], func=AF.Copy,
                scale=INV_D),
                reads=[RB[bq]], writes=[Rrows[1]], sem="act", inc=1)
            p.op("dve", lambda h: h.tensor_mul(r_t.ap(), r_sum.ap(), r_sum.ap()),
                 reads=[Rrows[0]], writes=[Rrows[2]], sem="dve", inc=1)
            p.op("dve", lambda h: h.tensor_sub(r_t.ap(), r_sq.ap(), r_t.ap()),
                 reads=[Rrows[1], Rrows[2]], writes=[Rrows[2]], sem="dve", inc=1)
            p.op("act", lambda h: h.activation(out=r_t.ap(), in_=r_t.ap(),
                                               func=AF.Ln, bias=EPS),
                 reads=[Rrows[2], Reps], writes=[Rrows[2]], sem="act", inc=1)
            p.op("act", lambda h, rstd_a=rstd_a: h.activation(
                out=rstd_a, in_=r_t.ap(), func=AF.Exp, scale=-0.5),
                reads=[Rrows[2]], writes=[Rrstd2[pr]], sem="act", inc=1)
            p.op("dve", lambda h, rstd_a=rstd_a, nmr_a=nmr_a:
                 h.scalar_tensor_tensor(nmr_a, r_sum.ap(), -1.0, rstd_a,
                                        ALU.mult, ALU.mult),
                 reads=[Rrows[0], Rrstd2[pr]], writes=[Rnmr2[pr]],
                 sem="dve", inc=1)

        def bca(tt):
            ba, bb, pr = 4 + (tt % 2), 6 + (tt % 2), tt % 2
            rstd_a, nmr_a = r_rstd2[pr].ap(), r_nmr2[pr].ap()
            p.op("pe", lambda h, ba=ba, rstd_a=rstd_a: h.matmul(
                pb[ba].ap(), onesrow.ap(), rstd_a, start=True, stop=True),
                reads=[Rrstd2[pr], Rones], writes=[RB[ba]], sem="pe", inc=1)
            p.op("pe", lambda h, bb=bb, nmr_a=nmr_a: h.matmul(
                pb[bb].ap(), onesrow.ap(), nmr_a, start=True, stop=True),
                reads=[Rnmr2[pr], Rones], writes=[RB[bb]], sem="pe", inc=1)
            if nmout:
                p.op("act", lambda h, ba=ba, pr=pr: h.activation(
                    out=absb[pr].ap(), in_=pb[ba].ap(), func=AF.Copy),
                    reads=[RB[ba]], writes=[Rabsb[pr]], sem="act", inc=1)
                p.op("act", lambda h, bb=bb, pr=pr: h.activation(
                    out=bbsb[pr].ap(), in_=pb[bb].ap(), func=AF.Copy),
                    reads=[RB[bb]], writes=[Rbbsb[pr]], sem="act", inc=1)
                rn_tt = [RN[c][tt] for c in range(NCD)]

                def nmv(tt=tt):
                    return NM.ap().rearrange("p (c t) -> p c t", c=NCD)[
                        :, :, tt * TTOK:(tt + 1) * TTOK]
                p.op("dve", lambda h, tt=tt, pr=pr: h.tensor_mul(
                    nmv(tt), nmv(tt), absb[pr].ap().unsqueeze(1).broadcast_to(
                        [128, NCD, TTOK])),
                    reads=rn_tt + [Rabsb[pr]], writes=rn_tt, sem="dve", inc=1)
                p.op("dve", lambda h, tt=tt, pr=pr: h.tensor_add(
                    nmv(tt), nmv(tt), bbsb[pr].ap().unsqueeze(1).broadcast_to(
                        [128, NCD, TTOK])),
                    reads=rn_tt + [Rbbsb[pr]], writes=rn_tt, sem="dve", inc=1)
                if kind != "out":
                    for c in range(NCD):
                        wca = wcols.ap()[:, l * NCD + c: l * NCD + c + 1]
                        bca_ = bcols.ap()[:, l * NCD + c: l * NCD + c + 1]
                        p.op("dve", lambda h, c=c, tt=tt, wca=wca, bca_=bca_:
                             h.tensor_scalar(nm(c, tt), nm(c, tt), wca, bca_,
                                             ALU.mult, ALU.add),
                             reads=[RN[c][tt], Rwcols[l], Rbcols[l]],
                             writes=[RN[c][tt]], sem="dve", inc=1)
            else:
                p.op("act", lambda h, ba=ba, pr=pr: h.activation(
                    out=afsb[pr].ap(), in_=pb[ba].ap(), func=AF.Copy),
                    reads=[RB[ba]], writes=[Rafsb[pr]], sem="act", inc=1)
                p.op("act", lambda h, bb=bb, pr=pr: h.activation(
                    out=bfsb[pr].ap(), in_=pb[bb].ap(), func=AF.Copy),
                    reads=[RB[bb]], writes=[Rbfsb[pr]], sem="act", inc=1)

                def hv(c0, c1, tt=tt):
                    return Hb.ap().rearrange("p (c t) -> p c t", c=NCD)[
                        :, c0:c1, tt * TTOK:(tt + 1) * TTOK]

                def afb(c0, c1, pr=pr):
                    return afsb[pr].ap().unsqueeze(1).broadcast_to(
                        [128, c1 - c0, TTOK])

                def bfb(c0, c1, pr=pr):
                    return bfsb[pr].ap().unsqueeze(1).broadcast_to(
                        [128, c1 - c0, TTOK])
                for (eng, sm, c0, c1) in (("dve", "dve", 0, NCD),):
                    rh_s = [RH[c][tt] for c in range(c0, c1)]
                    p.op(eng, lambda h, c0=c0, c1=c1, tt=tt, pr=pr:
                         h.tensor_mul(hv(c0, c1, tt), hv(c0, c1, tt),
                                      afb(c0, c1, pr)),
                         reads=rh_s + [Rafsb[pr]], writes=rh_s, sem=sm, inc=1)
                    p.op(eng, lambda h, c0=c0, c1=c1, tt=tt, pr=pr:
                         h.tensor_add(hv(c0, c1, tt), hv(c0, c1, tt),
                                      bfb(c0, c1, pr)),
                         reads=rh_s + [Rbfsb[pr]], writes=rh_s, sem=sm, inc=1)

        for tt in range(NT):
            stats(tt)
            rows(tt)
            if tt >= 1:
                bca(tt - 1)
        bca(NT - 1)

    soff_ctr = [0]

    def next_piece(ncols):
        so = soff_ctr[0]
        soff_ctr[0] += ncols
        return emit_piece_dma(ncols, so)

    def emit_rwkv_layer(l, seq, kvr_pieces=None, post_do=None):
        p.fence([Rkv, Rst], Rabsb + Rbbsb + Rbfsb + Rafsb)
        emit_ln("in", l, seq, lambda c, tt: H(c, tt), lambda c, tt: [RH[c][tt]])
        p.fence(Rabsb + Rbbsb + Rbfsb + Rafsb, [Rkv, Rst])
        for e in range(NCD):
            if kvr_pieces and e < len(kvr_pieces):
                tile, rw_regs = kvr_pieces[e]
            else:
                tile, rw_regs = next_piece(3 * 1024)
            for tt in range(NT):
                for wi, wname in enumerate(("k", "v", "r")):
                    bank0 = {0: 0, 1: 2, 2: 4}[wi]
                    bank = bank0 + (tt % 2)
                    pieces = []
                    for dch in range(NCD):
                        lt = tile(wi * 1024 + dch * 128, wi * 1024 + (dch + 1) * 128)
                        pieces.append((lt, nm(dch, tt), rw_regs + [RN[dch][tt]]))
                    group(bank, TTOK, pieces)
                    if wi == 0:
                        p.op("act", lambda h, tt=tt, bank=bank: h.activation(
                            out=ks_t.ap()[:, tt * TTOK:(tt + 1) * TTOK],
                            in_=pb[bank].ap(), func=AF.Copy, scale=1.0 - DEC),
                            reads=[RB[bank]], writes=[Rks], sem="act", inc=1)
                    elif wi == 1:
                        p.op("dve", lambda h, tt=tt, bank=bank: h.tensor_mul(
                            kv_t.ap()[:, tt * TTOK:(tt + 1) * TTOK],
                            ks_t.ap()[:, tt * TTOK:(tt + 1) * TTOK],
                            pb[bank].ap()),
                            reads=[Rks, RB[bank]], writes=[Rkv], sem="dve", inc=1)
                    else:
                        p.op("act", lambda h, tt=tt, bank=bank: h.activation(
                            out=sr_t.ap()[:, tt * TTOK:(tt + 1) * TTOK],
                            in_=pb[bank].ap(), func=AF.Sigmoid),
                            reads=[RB[bank]], writes=[Rsr], sem="act", inc=1)
            p.op("dve", lambda h: h.tensor_tensor_scan(
                st_t.ap(), dec_t.ap().broadcast_to([128, T]), kv_t.ap(),
                0.0, ALU.mult, ALU.add),
                reads=[Rkv, Rdec], writes=[Rst], sem="dve", inc=1)
            p.op("dve", lambda h, e=e: h.tensor_mul(yv(e), sr_t.ap(), st_t.ap()),
                 reads=[Rsr, Rst], writes=[RY[e]], sem="dve", inc=1)
        if post_do is not None:
            p.fence([Rsr, Rks], [Rxstr, Rxstrb])
            for do in range(NCD):
                tile, wo_regs = next_piece(1024)
                for tt in range(NT):
                    bank = 6 + (tt % 2)
                    pieces = []
                    for e in range(NCD):
                        lt = tile(e * 128, (e + 1) * 128)
                        pieces.append((lt, yv(e, tt), wo_regs + [RY[e]]))
                    group(bank, TTOK, pieces)
                    p.op("dve", lambda h, do=do, tt=tt, bank=bank: h.tensor_add(
                        H(do, tt), H(do, tt), pb[bank].ap()),
                        reads=[RH[do][tt], RB[bank]], writes=[RH[do][tt]],
                        sem="dve", inc=1)
                post_do(do)
        else:
            wo_tiles = [next_piece(1024) for _ in range(NCD)]
            for tt in range(NT):
                for do in range(NCD):
                    tile, wo_regs = wo_tiles[do]
                    bank = 6 + (do % 2)
                    pieces = []
                    for e in range(NCD):
                        lt = tile(e * 128, (e + 1) * 128)
                        pieces.append((lt, yv(e, tt), wo_regs + [RY[e]]))
                    group(bank, TTOK, pieces)
                    p.op("dve", lambda h, do=do, tt=tt, bank=bank: h.tensor_add(
                        H(do, tt), H(do, tt), pb[bank].ap()),
                        reads=[RH[do][tt], RB[bank]], writes=[RH[do][tt]],
                        sem="dve", inc=1)

    def emit_router(seq):
        tile, rt_regs = next_piece(128)
        for tc in range(NTC):
            bank = 0 + (tc % 2)
            pieces = []
            for c in range(NCD):
                pieces.append((nm128(c, tc), tile(c * NE, (c + 1) * NE),
                               rt_regs + [RN[c][tc * 128 // TTOK]]))
            group(bank, NE, pieces)
            p.op("dve", lambda h, tc=tc, bank=bank: h.tensor_add(
                lg_all.ap()[:, tc * NE:(tc + 1) * NE], pb[bank].ap()[:, 0:NE],
                biasbc.ap()),
                reads=[RB[bank], Rbbc], writes=[Rlog], sem="dve", inc=1)
        L3 = lg_all.ap().rearrange("p (c e) -> p c e", e=NE)
        M13 = m16a.ap().unsqueeze(2).broadcast_to([128, NTC, NE])
        M23 = m16b.ap().unsqueeze(2).broadcast_to([128, NTC, NE])
        R13 = rw1_16.ap().unsqueeze(2).broadcast_to([128, NTC, NE])
        R23 = rw2_16.ap().unsqueeze(2).broadcast_to([128, NTC, NE])
        D3 = maskda.ap().rearrange("p (c e) -> p c e", e=NE)
        p.op("dve", lambda h: h.reduce_max(m16a.ap(), L3, axis=AXX),
             reads=[Rlog], writes=[Rtop], sem="dve", inc=1)
        p.op("dve", lambda h: h.tensor_tensor(mask1a.ap().rearrange(
            "p (c e) -> p c e", e=NE), L3, M13, ALU.is_ge),
            reads=[Rlog, Rtop], writes=[Rtop], sem="dve", inc=1)
        p.op("dve", lambda h: h.scalar_tensor_tensor(
            maskda.ap(), mask1a.ap(), -3.0e38, lg_all.ap(), ALU.mult, ALU.add),
            reads=[Rlog, Rtop], writes=[Rtop], sem="dve", inc=1)
        p.op("dve", lambda h: h.reduce_max(m16b.ap(), D3, axis=AXX),
             reads=[Rtop], writes=[Rtop], sem="dve", inc=1)
        p.op("dve", lambda h: h.tensor_tensor(tmpa.ap().rearrange(
            "p (c e) -> p c e", e=NE), D3, M23, ALU.is_ge),
            reads=[Rtop], writes=[Rtop], sem="dve", inc=1)
        p.op("dve", lambda h: h.tensor_sub(m16a.ap(), m16a.ap(), m16b.ap()),
             reads=[Rtop], writes=[Rtop], sem="dve", inc=1)
        p.op("act", lambda h: h.activation(out=rw1_16.ap(), in_=m16a.ap(),
                                           func=AF.Sigmoid),
             reads=[Rtop], writes=[Rtop], sem="act", inc=1)
        p.op("dve", lambda h: h.tensor_scalar(rw2_16.ap(), rw1_16.ap(), -1.0,
                                              1.0, ALU.mult, ALU.add),
             reads=[Rtop], writes=[Rtop], sem="dve", inc=1)
        p.op("dve", lambda h: h.tensor_tensor(mask1a.ap().rearrange(
            "p (c e) -> p c e", e=NE), mask1a.ap().rearrange(
            "p (c e) -> p c e", e=NE), R13, ALU.mult),
            reads=[Rtop], writes=[Rtop], sem="dve", inc=1)
        p.op("dve", lambda h: h.tensor_tensor(tmpa.ap().rearrange(
            "p (c e) -> p c e", e=NE), tmpa.ap().rearrange(
            "p (c e) -> p c e", e=NE), R23, ALU.mult),
            reads=[Rtop], writes=[Rtop], sem="dve", inc=1)
        p.op("dve", lambda h: h.tensor_add(cw_all.ap(), mask1a.ap(), tmpa.ap()),
             reads=[Rtop], writes=[Rcwall], sem="dve", inc=1)
        p.op("pe", lambda h: h.transpose(pb[2].ap()[:, 0:128], cw_all.ap(),
                                         ident.ap()),
             reads=[Rcwall, Rident], writes=[RB[2]], sem="pe", inc=1)
        p.op("act", lambda h: h.activation(out=cwT.ap(), in_=pb[2].ap()[:, 0:128],
                                           func=AF.Copy),
             reads=[RB[2]], writes=[RcwT], sem="act", inc=1)
        p.op("sp", lambda h: h.dma_start(out=scrD, in_=cwT.ap()),
             reads=[RcwT], writes=[RscrD], sem=p.dma_sem("sq"), inc=16)

    def emit_cw_prep(ex):
        src = scrD.rearrange("(c e) q -> e c q", e=NE)[ex].unsqueeze(0)
        dst = cw_row.ap().rearrange("p (c q) -> p c q", c=NTC)
        p.op("pool", lambda h, src=src, dst=dst: h.dma_start(out=dst, in_=src),
             reads=[RscrD], writes=[Rcwrow], sem=p.dma_sem("wq"), inc=16)
        for tt in range(NT):
            bank = 6 + (tt % 2)
            p.op("pe", lambda h, tt=tt, bank=bank: h.matmul(
                pb[bank].ap(), onesrowb.ap(),
                cw_row.ap()[:, tt * TTOK:(tt + 1) * TTOK], start=True, stop=True),
                reads=[Rcwrow, Ronesb], writes=[RB[bank]], sem="pe", inc=1)
            p.op("act", lambda h, tt=tt, bank=bank: h.activation(
                out=cw_bc.ap()[:, tt * TTOK:(tt + 1) * TTOK],
                in_=pb[bank].ap(), func=AF.Copy),
                reads=[RB[bank]], writes=[Rcwbc], sem="act", inc=1)

    def emit_expert(ex):
        if ex >= 0:
            emit_cw_prep(ex)
        for h1 in range(NCH):
            tile, w_regs = next_piece(2048)
            for tt in range(NT):
                bank = 0 + (tt % 2)
                pieces = []
                for dch in range(NCD):
                    lt = tile(dch * 128, (dch + 1) * 128)
                    pieces.append((lt, nm(dch, tt), w_regs + [RN[dch][tt]]))
                group(bank, TTOK, pieces)
                p.op("act", lambda h, tt=tt, bank=bank: h.activation(
                    out=silu_t[tt % 2].ap(), in_=pb[bank].ap(), func=AF.Silu),
                    reads=[RB[bank]], writes=[Rsilu[tt % 2]], sem="act", inc=1)
                banku = 2 + (tt % 2)
                pieces = []
                for dch in range(NCD):
                    lt = tile(1024 + dch * 128, 1024 + (dch + 1) * 128)
                    pieces.append((lt, nm(dch, tt), w_regs + [RN[dch][tt]]))
                group(banku, TTOK, pieces)
                p.op("dve", lambda h, h1=h1, tt=tt, banku=banku: h.tensor_mul(
                    eo(h1, tt), silu_t[tt % 2].ap(), pb[banku].ap()),
                    reads=[Rsilu[tt % 2], RB[banku]], writes=[REO[h1]],
                    sem="dve", inc=1)
        for do in range(NCD):
            tile, w_regs = next_piece(2048)
            for tt in range(NT):
                bank = 4 + (tt % 2)
                pieces = []
                for h1 in range(NCH):
                    lt = tile(h1 * 128, (h1 + 1) * 128)
                    pieces.append((lt, eo(h1, tt), w_regs + [REO[h1]]))
                group(bank, TTOK, pieces)
                if ex >= 0:
                    p.op("dve", lambda h, tt=tt, bank=bank: h.tensor_mul(
                        tmpm[tt % 2].ap(), pb[bank].ap(),
                        cw_bc.ap()[:, tt * TTOK:(tt + 1) * TTOK]),
                        reads=[RB[bank], Rcwbc], writes=[Rtmpm[tt % 2]],
                        sem="dve", inc=1)
                    p.op("dve", lambda h, do=do, tt=tt: h.tensor_add(
                        H(do, tt), H(do, tt), tmpm[tt % 2].ap()),
                        reads=[RH[do][tt], Rtmpm[tt % 2]], writes=[RH[do][tt]],
                        sem="dve", inc=1)
                else:
                    p.op("dve", lambda h, do=do, tt=tt, bank=bank: h.tensor_add(
                        H(do, tt), H(do, tt), pb[bank].ap()),
                        reads=[RH[do][tt], RB[bank]], writes=[RH[do][tt]],
                        sem="dve", inc=1)

    def emit_seq(seq):
        rwkv_regs = [RY[e] for e in range(NCD)] + [Rkv, Rst, Rsr, Rks, Rxstrb]
        row_regs = list(Rrows)
        ffn_regs = list(REO) + list(Rsilu) + list(Rtmpm) + [Rcwrow, Rcwbc]
        ab_regs = Rabsb + Rbbsb + Rbfsb + Rafsb
        p.fence(ffn_regs, rwkv_regs + row_regs + ab_regs + [Rxstr])

        l0_pieces = [next_piece(3 * 1024) for _ in range(2)]
        for c in range(NCD):
            srcx = xin[seq].rearrange("(c q) t -> c q t", c=NCD)[c]
            p.op("sp", lambda h, c=c, srcx=srcx: h.dma_start(out=H(c), in_=srcx),
                 reads=[], writes=list(RH[c]), sem=p.dma_sem("sq"), inc=16)
        emit_ln("ln1", 0, seq, lambda c, tt: H(c, tt), lambda c, tt: [RH[c][tt]])
        if seq == 0:
            tap("h0", Hb.ap(), sum(RH, []))

        def final_add(do):
            buf = (xstr, Rxstr) if do % 2 == 0 else (xstr_b, Rxstrb)
            srcx = xin[seq].rearrange("(c q) t -> c q t", c=NCD)[do]
            p.op("sp", lambda h, srcx=srcx, buf=buf: h.dma_start(
                out=buf[0].ap(), in_=srcx),
                reads=[], writes=[buf[1]], sem=p.dma_sem("sq"), inc=16)
            p.op("dve", lambda h, do=do, buf=buf: h.tensor_add(
                H(do), H(do), buf[0].ap()),
                reads=RH[do] + [buf[1]], writes=list(RH[do]), sem="dve", inc=1)

        for l in range(NLAY):
            emit_rwkv_layer(l, seq, kvr_pieces=l0_pieces if l == 0 else None,
                            post_do=final_add if l == NLAY - 1 else None)
            if seq == 0:
                tap("hl%d" % l, Hb.ap(), sum(RH, []))
                if l == 0:
                    tap("nm0", NM.ap(), sum(RN, []))
                    tap("y0", y_t.ap(), list(RY))
                    tap("ks7", ks_t.ap(), [Rks])
                    tap("kv7", kv_t.ap(), [Rkv])
                    tap("st7", st_t.ap(), [Rst])
                    tap("sr7", sr_t.ap(), [Rsr])
        if seq == 0:
            tap("x1", Hb.ap(), sum(RH, []))
        p.fence([Rkv, Rst], Rabsb + Rbbsb + Rbfsb + Rafsb)
        emit_ln("out", 0, seq, lambda c, tt: H(c, tt), lambda c, tt: [RH[c][tt]])
        if seq == 0:
            tap("nm2", NM.ap(), sum(RN, []))
        emit_router(seq)
        if seq == 0:
            tap("cwa", cw_all.ap(), [Rcwall])
        p.fence(rwkv_regs + row_regs + ab_regs + [Rxstr], ffn_regs)
        for ex in range(-1, NE):
            emit_expert(ex)
            if seq == 0 and ex == -1:
                tap("hsh", Hb.ap(), sum(RH, []))
        for c in range(NCD):
            dst = xout[seq].rearrange("(c q) t -> c q t", c=NCD)[c]
            p.op("sp", lambda h, c=c, dst=dst: h.dma_start(out=dst, in_=H(c)),
                 reads=list(RH[c]), writes=[], sem=p.dma_sem("sq"), inc=16)

    init()
    for seq in range(BL):
        soff_ctr[0] = 0
        emit_seq(seq)
    p.wait("sp", {f"sq{i}": p.cnt[f"sq{i}"] for i in range(NSQ)}, force=True)

    def runner(qname):
        def f(h):
            for emit, waits, mark in p.q[qname]:
                for s, c in waits:
                    h.wait_ge(sems[s], c)
                if emit is not None:
                    inst = emit(h)
                    if mark is not None:
                        inst.then_inc(sems[mark[0]], mark[2])
        return f

    with nc.Block() as blk:
        blk.sync(runner("sp"))
        blk.gpsimd(runner("pool"))
        blk.tensor(runner("pe"))
        blk.vector(runner("dve"))
        blk.scalar(runner("act"))

    es.close()
    return nc


_NC_CACHE = {}


def kernel(**inputs):
    x = np.ascontiguousarray(np.asarray(inputs["L0_attn_h"], np.float32))
    ws = pack_stream(inputs)
    lnw = np.asarray(inputs["L0_attn_ln_w"], np.float32)
    lnb = np.asarray(inputs["L0_attn_ln_b"], np.float32)
    bias = np.asarray(inputs["L0_ffn_experts_bias"], np.float32)
    lnw_cols = np.ascontiguousarray(lnw.reshape(NLAY, NCD, 128).transpose(0, 2, 1))
    lnb_cols = np.ascontiguousarray(lnb.reshape(NLAY, NCD, 128).transpose(0, 2, 1))
    ident = np.eye(128, dtype=np.float32)
    bias_row = np.ascontiguousarray(bias.reshape(1, NE))

    xbdt = np.ascontiguousarray(x.transpose(0, 2, 1))

    key = "nc_dbg" if DEBUG_TAPS else "nc"
    if key not in _NC_CACHE:
        _NC_CACHE[key] = build_nc(debug=DEBUG_TAPS)
    nc = _NC_CACHE[key]

    in_maps = []
    for c in range(NCORE):
        in_maps.append({
            "xin": np.ascontiguousarray(xbdt[c * BL:(c + 1) * BL]),
            "wstream": ws,
            "ident": ident,
            "lnwcols": lnw_cols,
            "lnbcols": lnb_cols,
            "biasrow": bias_row,
        })
    res = run_bass_kernel_spmd(nc, in_maps, core_ids=list(range(NCORE)))
    out = np.empty((B, T, D), np.float32)
    for c in range(NCORE):
        o = np.asarray(res.results[c]["xout"], np.float32)
        out[c * BL:(c + 1) * BL] = o.transpose(0, 2, 1)
    if DEBUG_TAPS:
        kernel._last_taps = [res.results[c] for c in range(NCORE)]
    return out
